# version 02
# baseline (speedup 1.0000x reference)
import contextlib
import numpy as np
import concourse.bass as bass
import concourse.mybir as mybir
from concourse.bass_utils import run_bass_kernel_spmd

F32 = mybir.dt.float32
BF16 = mybir.dt.bfloat16
AF = mybir.ActivationFunctionType
ALU = mybir.AluOpType
AX = mybir.AxisListType

MAGIC = 12582912.0
VELT = 1048577.0
SQRT2 = float(np.float32(np.sqrt(np.float64(2.0))))
Q_SQRT2 = float(np.float32(0.25 / np.sqrt(np.float64(2.0))))
T = 2048
D = 1024
NT = 16
NPAIR = 8
QT = 4
INV448 = 1.0 / 448.0
EPS_S = 1e-12
LN_EPS = 1e-5


class Sem:
    __slots__ = ("h", "n")

    def __init__(self, h):
        self.h = h
        self.n = 0


class Rec:
    __slots__ = ("ops",)

    def __init__(self):
        self.ops = []

    def wait(self, sem, val=None):
        v = sem.n if val is None else val
        self.ops.append(lambda e, h=sem.h, v=v: e.wait_ge(h, v))

    def op(self, method, *args, inc=None, k=1, **kw):
        if inc is not None:
            inc.n += k

        def f(e, m=method, a=args, kw=kw, inc=inc, k=k):
            r = getattr(e, m)(*a, **kw)
            if inc is not None:
                r.then_inc(inc.h, k)
        self.ops.append(f)

    def dma(self, out, in_, inc, transpose=False):
        inc.n += 16

        def f(e, out=out, in_=in_, inc=inc, tr=transpose):
            if tr:
                e.dma_start(out=out, in_=in_, transpose=True).then_inc(inc.h, 16)
            else:
                e.dma_start(out=out, in_=in_).then_inc(inc.h, 16)
        self.ops.append(f)


def build(dbg_tag=None, dbg_cols=2048, dbg_bf16=False, upto=None):
    nc = bass.Bass("TRN2", target_bir_lowering=False)

    x_d = nc.dram_tensor("x", [NT, 128, D], F32, kind="ExternalInput").ap()
    wq_d = nc.dram_tensor("wqkv", [24, 8, 128, 128], F32, kind="ExternalInput").ap()
    bq_d = nc.dram_tensor("bqkv", [128, 24], F32, kind="ExternalInput").ap()
    wo_d = nc.dram_tensor("wout", [8, 2, 128, 512], F32, kind="ExternalInput").ap()
    bo_d = nc.dram_tensor("bout", [1, D], F32, kind="ExternalInput").ap()
    w1_d = nc.dram_tensor("w1", [32, 8, 128, 128], F32, kind="ExternalInput").ap()
    b1_d = nc.dram_tensor("b1", [128, 32], F32, kind="ExternalInput").ap()
    w2_d = nc.dram_tensor("w2", [32, 2, 128, 512], F32, kind="ExternalInput").ap()
    b2_d = nc.dram_tensor("b2", [1, D], F32, kind="ExternalInput").ap()
    id_d = nc.inline_tensor(np.eye(128, dtype=np.float32), name="ident").ap()
    out_d = nc.dram_tensor("out", [NT, 128, D], F32, kind="ExternalOutput").ap()
    dbg_d = None
    if dbg_tag is not None:
        dbg_d = nc.dram_tensor("dbg", [128, dbg_cols],
                               BF16 if dbg_bf16 else F32,
                               kind="ExternalOutput").ap()

    es = contextlib.ExitStack()
    with es:
        def sb(name, shape, dt):
            return es.enter_context(nc.sbuf_tensor(name, shape, dt))

        x_sb = sb("x_sb", [128, NT * D], F32)
        at_sb = sb("at_sb", [128, 8 * T], BF16)
        sbf = sb("sbf", [128, 32768], BF16)
        sf = sb("sf", [128, 5120], F32)
        scrA = sb("scrA", [128, D], F32)
        scrB = sb("scrB", [128, D], F32)
        scrC = sb("scrC", [128, D], F32)
        scrD = sb("scrD", [128, D], F32)
        stats = sb("stats", [128, 64], F32)
        wst = sb("wst", [128, 2 * 1024], BF16)
        wf32 = sb("wf32", [128, 1536], F32)
        bq_sb = sb("bq_sb", [128, 24], F32)
        b1c_sb = sb("b1c_sb", [128, 32], F32)
        id_sb = sb("id_sb", [128, 128], F32)
        on1_sb = sb("on1_sb", [1, 128], BF16)
        onesA = sb("onesA", [128, 1], F32)
        ngc = sb("ngc", [128, 1], F32)
        half1 = sb("half1", [1, 1], BF16)

        QP_O, KP_O = 0, 2048
        VTA_O, VTB_O = 4096, 5632
        AT_A, AT_B = 7168, 8192
        OTQ_O = 9216
        OTALL_O = 11264
        ATOK_O = 27648
        WOUT_O = 0
        YQ_O = 0

        OTS_O = 0
        OTOK_O = 1024
        OPK_O = 2112
        FFO_O = 4160
        QSQ_O = 4608
        PW_O = 0

        psum = [es.enter_context(nc.psum_tensor(f"ps{i}", [128, 512], F32))
                for i in range(8)]

        S = {n: Sem(es.enter_context(nc.semaphore(n)))
             for n in ["sLD", "sTR", "sST", "sV", "sS", "sT", "sG"]}
        sLD, sTR, sST = S["sLD"], S["sTR"], S["sST"]
        sV, sS, sT_, sG = S["sV"], S["sS"], S["sT"], S["sG"]

        block = es.enter_context(nc.Block())
        rs, rv, rc, rt, rg = Rec(), Rec(), Rec(), Rec(), Rec()

        def finish():
            rs.wait(sST)
            for name, rec in (("sync", rs), ("vector", rv), ("scalar", rc),
                              ("tensor", rt), ("gpsimd", rg)):
                getattr(block, name)(lambda e, ops=rec.ops: [f(e) for f in ops])

        def dbg_dump(tag, src):
            if dbg_tag is None or dbg_tag != tag:
                return
            rs.wait(sV)
            rs.wait(sS)
            rs.wait(sT_)
            rs.wait(sG)
            rs.wait(sTR)
            rs.wait(sST)
            rs.dma(dbg_d[:], src, sST)

        for i in range(NT):
            rs.dma(x_sb[:, i * D:(i + 1) * D], x_d[i], sLD)
        rs.dma(bq_sb[:], bq_d[:], sLD)
        rs.dma(b1c_sb[:], b1_d[:], sLD)
        rs.dma(id_sb[:], id_d[:], sLD)

        rv.wait(sLD)
        rv.op("memset", on1_sb[:], 1.0)
        rv.op("memset", onesA[:], 1.0)
        rv.op("engine_nop", inc=sV)

        if upto == "p0":
            dbg_dump(dbg_tag, x_sb[:, 0:dbg_cols])
            finish()
            return nc

        def layernorm_tiles(writer, do_quant):
            for i in range(NT):
                xt = x_sb[:, i * D:(i + 1) * D]
                rv.op("reduce_sum", stats[:, 0:1], xt, AX.X, inc=sV)
                rv.wait(sV)
                rv.op("tensor_scalar", stats[:, 1:2], stats[:, 0:1],
                      1.0 / D, None, ALU.mult, inc=sV)
                rv.wait(sV)
                rv.op("tensor_scalar", scrA[:], xt, stats[:, 1:2], None,
                      ALU.subtract, inc=sV)
                rc.wait(sV)
                rc.op("activation", scrB[:], scrA[:], AF.Square, scale=1.0,
                      inc=sS)
                rv.wait(sS)
                rv.op("reduce_sum", stats[:, 2:3], scrB[:], AX.X, inc=sV)
                rv.wait(sV)
                rv.op("tensor_scalar", stats[:, 3:4], stats[:, 2:3],
                      1.0 / D, LN_EPS, ALU.mult, ALU.add, inc=sV)
                rc.wait(sV)
                rc.op("activation", stats[:, 4:5], stats[:, 3:4], AF.Sqrt,
                      scale=1.0, inc=sS)
                rv.wait(sS)
                rv.op("reciprocal", stats[:, 5:6], stats[:, 4:5], inc=sV)
                rv.wait(sV)
                rv.op("tensor_scalar", scrB[:], scrA[:], stats[:, 5:6], None,
                      ALU.mult)
                if do_quant:
                    rv.op("tensor_reduce", stats[:, 8:16],
                          scrB[:].rearrange("p (s n) -> p s n", s=8),
                          AX.X, ALU.max, apply_absolute_value=True, inc=sV)
                    rv.wait(sV)
                    rv.op("tensor_scalar", stats[:, 16:24], stats[:, 8:16],
                          INV448, EPS_S, ALU.mult, ALU.max, inc=sV)
                    rv.wait(sV)
                    rv.op("reciprocal", stats[:, 24:32], stats[:, 16:24], inc=sV)
                    rv.wait(sV)
                    for sg in range(8):
                        rv.op("tensor_scalar",
                              scrA[:, sg * 128:(sg + 1) * 128],
                              scrB[:, sg * 128:(sg + 1) * 128],
                              stats[:, 24 + sg:25 + sg], None, ALU.mult)
                    rv.op("tensor_scalar", scrD[:], scrA[:], VELT, None,
                          ALU.mult)
                    rv.op("tensor_tensor", scrC[:], scrD[:], scrA[:],
                          ALU.subtract)
                    rv.op("tensor_tensor", scrC[:], scrD[:], scrC[:],
                          ALU.subtract)
                    for sg in range(8):
                        rv.op("tensor_scalar",
                              scrB[:, sg * 128:(sg + 1) * 128],
                              scrC[:, sg * 128:(sg + 1) * 128],
                              stats[:, 16 + sg:17 + sg], None, ALU.mult)
                writer(i, scrB)

        def make_at_writer():
            marks = []

            def writer(i, src):
                slot = ATOK_O + (i % 2) * 1024
                if len(marks) >= 2:
                    rv.wait(sTR, marks[-2])
                rv.op("tensor_copy", sbf[:, slot:slot + D], src[:], inc=sV)
                rc.wait(sV)
                for ch in range(8):
                    rc.dma(at_sb[:, ch * T + i * 128: ch * T + (i + 1) * 128],
                           sbf[:, slot + ch * 128: slot + (ch + 1) * 128],
                           sTR, transpose=True)
                marks.append(sTR.n)
            return writer

        layernorm_tiles(make_at_writer(), do_quant=True)
        dbg_dump("at", at_sb[:, 0:dbg_cols])
        if upto == "ln1":
            finish()
            return nc

        wstate = {"wf32a": 0, "wf32b": 0, "wst0": 0, "wst1": 0}

        def prep_qkv_style(dram_ap, buf, quant):
            rs.wait(sV, wstate["wf32a"])
            rs.dma(wf32[:, 0:1024].rearrange("p (dc f) -> p dc f", dc=8),
                   dram_ap.rearrange("dc p f -> p dc f"), sLD)
            rv.wait(sLD)
            rv.wait(sT_, wstate["wst%d" % buf])
            if not quant:
                rv.op("tensor_copy", wst[:, buf * 1024:(buf + 1) * 1024],
                      wf32[:, 0:1024], inc=sV)
                wstate["wf32a"] = sV.n
                return
            for dc in range(8):
                w = wf32[:, dc * 128:(dc + 1) * 128]
                rv.op("tensor_reduce", stats[:, 32:33], w, AX.X, ALU.max,
                      apply_absolute_value=True, inc=sV)
                rv.wait(sV)
                rv.op("tensor_scalar", stats[:, 33:34], stats[:, 32:33],
                      INV448, EPS_S, ALU.mult, ALU.max, inc=sV)
                rv.wait(sV)
                rv.op("reciprocal", stats[:, 34:35], stats[:, 33:34], inc=sV)
                rv.wait(sV)
                rv.op("tensor_scalar", scrD[:, 0:128], w, stats[:, 34:35],
                      None, ALU.mult)
                rv.op("tensor_scalar", scrD[:, 128:256], scrD[:, 0:128],
                      VELT, None, ALU.mult)
                rv.op("tensor_tensor", scrD[:, 256:384], scrD[:, 128:256],
                      scrD[:, 0:128], ALU.subtract)
                rv.op("tensor_tensor", scrD[:, 256:384], scrD[:, 128:256],
                      scrD[:, 256:384], ALU.subtract)
                rv.op("tensor_scalar",
                      wst[:, buf * 1024 + dc * 128: buf * 1024 + (dc + 1) * 128],
                      scrD[:, 256:384], stats[:, 33:34], None, ALU.mult,
                      inc=sV)
            wstate["wf32a"] = sV.n

        def quant_mv_chunk(w, dst):
            rv.op("tensor_reduce", stats[:, 36:40],
                  w.rearrange("p (s n) -> p s n", s=4), AX.X, ALU.max,
                  apply_absolute_value=True, inc=sV)
            rv.wait(sV)
            rv.op("tensor_scalar", stats[:, 40:44], stats[:, 36:40],
                  INV448, EPS_S, ALU.mult, ALU.max, inc=sV)
            rv.wait(sV)
            rv.op("reciprocal", stats[:, 44:48], stats[:, 40:44], inc=sV)
            rv.wait(sV)
            for sg in range(4):
                rv.op("tensor_scalar", scrD[:, sg * 128:(sg + 1) * 128],
                      w[:, sg * 128:(sg + 1) * 128],
                      stats[:, 44 + sg:45 + sg], None, ALU.mult)
            rv.op("tensor_scalar", scrD[:, 512:1024], scrD[:, 0:512],
                  VELT, None, ALU.mult)
            rv.op("tensor_tensor", scrC[:, 0:512], scrD[:, 512:1024],
                  scrD[:, 0:512], ALU.subtract)
            rv.op("tensor_tensor", scrC[:, 0:512], scrD[:, 512:1024],
                  scrC[:, 0:512], ALU.subtract)
            for sg in range(4):
                rv.op("tensor_scalar",
                      dst[:, sg * 128:(sg + 1) * 128],
                      scrC[:, sg * 128:(sg + 1) * 128],
                      stats[:, 40 + sg:41 + sg], None, ALU.mult,
                      inc=(sV if sg == 3 else None))

        at_ready = sTR.n
        first_qkv = True
        for p in range(NPAIR):
            vtr_marks = []
            for which, oc in ((0, p), (1, 8 + p), (2, 16 + p)):
                dst_off = (QP_O, KP_O, None)[which]
                prep_qkv_style(wq_d[oc], oc % 2, quant=True)
                stat_mark = sV.n
                for q4 in range(QT):
                    pb = psum[q4 % 2]
                    for dc in range(8):
                        if dc == 0:
                            rt.wait(sV, stat_mark)
                            rt.wait(sV)
                            if first_qkv:
                                rt.wait(sTR, at_ready)
                                first_qkv = False
                        rt.op("matmul", pb[:],
                              wst[:, (oc % 2) * 1024 + dc * 128:
                                  (oc % 2) * 1024 + (dc + 1) * 128],
                              at_sb[:, dc * T + q4 * 512: dc * T + (q4 + 1) * 512],
                              start=(dc == 0), stop=(dc == 7),
                              inc=(sT_ if dc == 7 else None))
                    if q4 == QT - 1:
                        wstate["wst%d" % (oc % 2)] = sT_.n
                    rv.wait(sT_)
                    if which == 2:
                        stg = AT_A + (q4 % 2) * 512
                        if len(vtr_marks) >= 2:
                            rv.wait(sTR, vtr_marks[-2])
                        rv.op("tensor_scalar", sbf[:, stg:stg + 512], pb[:],
                              bq_sb[:, oc:oc + 1], None, ALU.add, inc=sV)
                        rs.wait(sV)
                        for j in range(4):
                            kt = q4 * 4 + j
                            vsrc = sbf[:, stg + j * 128: stg + (j + 1) * 128]
                            rs.dma(sbf[:, VTA_O + kt * 96: VTA_O + kt * 96 + 64],
                                   vsrc[0:64, :], sTR, transpose=True)
                            rs.dma(sbf[:, VTB_O + kt * 96: VTB_O + kt * 96 + 64],
                                   vsrc[64:128, :], sTR, transpose=True)
                        vtr_marks.append(sTR.n)
                    else:
                        rv.op("tensor_scalar",
                              sbf[:, dst_off + q4 * 512: dst_off + (q4 + 1) * 512],
                              pb[:], bq_sb[:, oc:oc + 1], None, ALU.add,
                              inc=sV)
            vtr_mark = sTR.n
            rv.wait(sTR, vtr_mark)
            for kt in range(NT):
                rv.op("memset",
                      sbf[:, VTA_O + kt * 96 + 64: VTA_O + kt * 96 + 65], 1.0)
                rv.op("memset",
                      sbf[:, VTB_O + kt * 96 + 64: VTB_O + kt * 96 + 65], 1.0,
                      inc=(sV if kt == NT - 1 else None))
            dbg_dump(f"qkvq{p}", sbf[:, QP_O:QP_O + min(dbg_cols, 2048)])
            dbg_dump(f"qkvv{p}", sbf[:, VTA_O:VTA_O + min(dbg_cols, 1536)])
            if upto == "qkv0" and p == 0:
                finish()
                return nc

            for idx, src_off in ((0, QP_O), (1, KP_O)):
                for q4 in range(QT):
                    bank = 6 + idx
                    rc.wait(sT_)
                    rc.wait(sV)
                    rc.op("activation", sf[:, QSQ_O:QSQ_O + 512],
                          sbf[:, src_off + q4 * 512: src_off + (q4 + 1) * 512],
                          AF.Square, scale=1.0, inc=sS)
                    rt.wait(sS)
                    for hh in range(2):
                        rt.op("matmul", psum[bank][hh * 64:hh * 64 + 1, :],
                              onesA[hh * 64:(hh + 1) * 64, :],
                              sf[hh * 64:(hh + 1) * 64, QSQ_O:QSQ_O + 512],
                              start=(q4 == 0), stop=(q4 == QT - 1),
                              tile_position=(hh * 64, hh * 64),
                              skip_group_check=True,
                              inc=(sT_ if hh == 1 else None))
            rv.wait(sT_)
            for bi, bank in enumerate((6, 7)):
                for row in (0, 64):
                    rv.op("tensor_reduce", stats[row:row + 1, 48 + bi:49 + bi],
                          psum[bank][row:row + 1, :], AX.X, ALU.max, inc=sV)
                    rv.wait(sV)
            rt.wait(sV)
            rt.op("transpose", psum[2][0:1, 0:128], stats[:, 48:49], id_sb[:],
                  inc=sT_)
            rt.op("transpose", psum[3][0:1, 0:128], stats[:, 49:50], id_sb[:],
                  inc=sT_)
            rv.wait(sT_)
            rv.op("tensor_copy", sf[0:1, 0:128], psum[2][0:1, 0:128])
            rv.op("tensor_copy", sf[0:1, 128:256], psum[3][0:1, 0:128], inc=sV)
            rv.wait(sV)
            rv.op("tensor_tensor", sf[0:1, 240:241], sf[0:1, 0:1],
                  sf[0:1, 64:65], ALU.max)
            rv.op("tensor_tensor", sf[0:1, 242:243], sf[0:1, 128:129],
                  sf[0:1, 192:193], ALU.max, inc=sV)
            rv.wait(sV)
            rv.op("tensor_tensor", sf[0:1, 244:245], sf[0:1, 240:241],
                  sf[0:1, 242:243], ALU.mult, inc=sV)
            rc.wait(sV)
            rc.op("activation", sf[0:1, 245:246], sf[0:1, 244:245],
                  AF.Sqrt, scale=1.0, inc=sS)
            rv.wait(sS)
            rv.op("tensor_scalar", sf[0:1, 245:246], sf[0:1, 245:246],
                  0.125, 80.0, ALU.mult, ALU.min, inc=sV)
            rv.wait(sV)
            rv.op("tensor_scalar", half1[:], sf[0:1, 245:246], -1.0, None,
                  ALU.mult, inc=sV)
            rt.wait(sV)
            rt.op("matmul", psum[6][:, 8:9], on1_sb[:], half1[:],
                  start=True, stop=True, inc=sT_)
            rv.wait(sT_)
            rv.op("tensor_copy", ngc[:], psum[6][:, 8:9], inc=sV)
            ngc_mark = sV.n
            if upto == "bound0" and p == 0:
                dbg_dump("c1row", sf[:, 0:256])
                dbg_dump("ngcd", ngc[:])
                dbg_dump("nrm", stats[:, 48:50])
                finish()
                return nc

            for q4 in range(QT):
                av_free_mark = sV.n
                exp_marks = []

                def emit_scores(kt):
                    sbA = psum[2 + (kt % 2)]
                    sbB = psum[6 + (kt % 2)]
                    if kt >= 2:
                        rt.wait(sS, exp_marks[kt - 2])
                    else:
                        rt.wait(sV, av_free_mark)
                    rt.op("matmul", sbA[:],
                          sbf[0:64, KP_O + kt * 128: KP_O + (kt + 1) * 128],
                          sbf[0:64, QP_O + q4 * 512: QP_O + (q4 + 1) * 512],
                          start=True, stop=True, tile_position=(0, 0))
                    rt.op("matmul", sbB[:],
                          sbf[64:128, KP_O + kt * 128: KP_O + (kt + 1) * 128],
                          sbf[64:128, QP_O + q4 * 512: QP_O + (q4 + 1) * 512],
                          start=True, stop=True, tile_position=(64, 0),
                          inc=sT_)

                def emit_exp(kt):
                    sbA = psum[2 + (kt % 2)]
                    sbB = psum[6 + (kt % 2)]
                    rc.wait(sT_)
                    if kt == 0:
                        rc.wait(sV, ngc_mark)
                        rc.wait(sTR, vtr_mark)
                    rc.op("activation",
                          sbf[:, AT_A + (kt % 2) * 512: AT_A + (kt % 2) * 512 + 512],
                          sbA[:], AF.Exp, scale=0.125, bias=ngc[:])
                    rc.op("activation",
                          sbf[:, AT_B + (kt % 2) * 512: AT_B + (kt % 2) * 512 + 512],
                          sbB[:], AF.Exp, scale=0.125, bias=ngc[:], inc=sS)
                    exp_marks.append(sS.n)

                def emit_av(kt):
                    rt.wait(sS, exp_marks[kt])
                    rt.op("matmul", psum[4][0:65, :],
                          sbf[:, VTA_O + kt * 96: VTA_O + kt * 96 + 65],
                          sbf[:, AT_A + (kt % 2) * 512: AT_A + (kt % 2) * 512 + 512],
                          start=(kt == 0), stop=(kt == NT - 1),
                          skip_group_check=True)
                    rt.op("matmul", psum[5][0:65, :],
                          sbf[:, VTB_O + kt * 96: VTB_O + kt * 96 + 65],
                          sbf[:, AT_B + (kt % 2) * 512: AT_B + (kt % 2) * 512 + 512],
                          start=(kt == 0), stop=(kt == NT - 1),
                          skip_group_check=True,
                          inc=(sT_ if kt == NT - 1 else None))

                emit_scores(0)
                emit_exp(0)
                for kt in range(1, NT):
                    emit_scores(kt)
                    emit_av(kt - 1)
                    emit_exp(kt)
                emit_av(NT - 1)
                rv.wait(sT_)
                rv.op("tensor_copy", sf[0:65, OTS_O:OTS_O + 512], psum[4][0:65, :])
                rv.op("tensor_copy", sf[0:65, OTS_O + 512:OTS_O + 1024],
                      psum[5][0:65, :], inc=sV)
                if upto == "av0" and p == 0 and q4 == 0:
                    dbg_dump("oraw", sf[:, OTS_O:OTS_O + 1024])
                    finish()
                    return nc
                tr_mark = sV.n
                for hh, ofs in ((0, OTS_O), (1, OTS_O + 512)):
                    for j in range(4):
                        kt = q4 * 4 + j
                        rt.wait(sV)
                        rt.op("transpose", psum[2 + hh][:, 0:65],
                              sf[0:65, ofs + j * 128: ofs + (j + 1) * 128],
                              id_sb[0:65, 0:65], inc=sT_)
                        rv.wait(sT_)
                        rv.op("tensor_copy",
                              sf[:, OTOK_O + kt * 66: OTOK_O + kt * 66 + 65],
                              psum[2 + hh][:, 0:65], inc=sV)
                        rv.wait(sV)
                        rv.op("reciprocal", stats[:, 52:53],
                              sf[:, OTOK_O + kt * 66 + 64: OTOK_O + kt * 66 + 65], inc=sV)
                        rv.wait(sV)
                        rv.op("tensor_scalar",
                              sf[:, OPK_O + kt * 128 + hh * 64:
                                 OPK_O + kt * 128 + hh * 64 + 64],
                              sf[:, OTOK_O + kt * 66: OTOK_O + kt * 66 + 64],
                              stats[:, 52:53], None, ALU.mult, inc=sV)
                if upto == "otr0" and p == 0 and q4 == 0:
                    dbg_dump("opkq", sf[:, OPK_O:OPK_O + 512])
                    dbg_dump("otokq", sf[:, OTOK_O:OTOK_O + 1056])
                    finish()
                    return nc
            dbg_dump(f"opk{p}", sf[:, OPK_O:OPK_O + min(dbg_cols, 2048)])

            opk = sf[:, OPK_O:OPK_O + 2048]
            rv.op("tensor_reduce", stats[:, 8:24],
                  opk.rearrange("p (s n) -> p s n", s=16), AX.X, ALU.max,
                  apply_absolute_value=True, inc=sV)
            rv.wait(sV)
            rv.op("tensor_scalar", stats[:, 24:40], stats[:, 8:24],
                  INV448, EPS_S, ALU.mult, ALU.max, inc=sV)
            rv.wait(sV)
            rv.op("reciprocal", stats[:, 40:56], stats[:, 24:40], inc=sV)
            rv.wait(sV)
            for sg in range(16):
                dsty = (sf[:, OTOK_O + sg * 128: OTOK_O + (sg + 1) * 128]
                        if sg < 8 else
                        sf[:, OTS_O + (sg - 8) * 128: OTS_O + (sg - 7) * 128])
                rv.op("tensor_scalar", dsty, opk[:, sg * 128:(sg + 1) * 128],
                      stats[:, 40 + sg:41 + sg], None, ALU.mult)
            for hb, yo in ((0, OTOK_O), (1, OTS_O)):
                ysl = sf[:, yo:yo + 1024]
                rv.op("tensor_scalar", scrD[:], ysl, VELT, None, ALU.mult)
                rv.op("tensor_tensor", scrC[:], scrD[:], ysl, ALU.subtract)
                rv.op("tensor_tensor", scrC[:], scrD[:], scrC[:], ALU.subtract)
                for sg in range(8):
                    g = hb * 8 + sg
                    rv.op("tensor_scalar",
                          sbf[:, OTQ_O + g * 128: OTQ_O + (g + 1) * 128],
                          scrC[:, sg * 128:(sg + 1) * 128],
                          stats[:, 24 + g:25 + g], None, ALU.mult,
                          inc=(sV if g == 15 else None))
            rs.wait(sV)
            for tt in range(NT):
                rs.dma(sbf[:, OTALL_O + p * T + tt * 128:
                           OTALL_O + p * T + (tt + 1) * 128],
                       sbf[:, OTQ_O + tt * 128: OTQ_O + (tt + 1) * 128],
                       sTR, transpose=True)
            dbg_dump(f"otq{p}", sbf[:, OTQ_O:OTQ_O + min(dbg_cols, 2048)])
            if upto == "pair0" and p == 0:
                finish()
                return nc

        rv.wait(sT_)
        wall = sbf[:, WOUT_O:WOUT_O + 16 * 512]
        for pc in range(8):
            for dh in range(2):
                k = pc * 2 + dh
                wslot = wf32[:, 1024:1536]
                rs.wait(sV, wstate["wf32b"])
                rs.dma(wslot, wo_d[pc, dh], sLD)
                rv.wait(sLD)
                quant_mv_chunk(wslot, wall[:, k * 512:(k + 1) * 512])
                wstate["wf32b"] = sV.n
        rs.wait(sV)
        rs.dma(scrA[:], bo_d.broadcast_to([128, D]), sLD)
        rv.wait(sLD)
        rv.op("engine_nop", inc=sV)
        wout_mark = sV.n
        rt.wait(sTR)
        for tt in range(NT):
            for dh in range(2):
                pb = psum[(tt * 2 + dh) % 2]
                for pc in range(8):
                    if pc == 0:
                        rt.wait(sV, wout_mark)
                        rt.wait(sV)
                    rt.op("matmul", pb[:],
                          sbf[:, OTALL_O + pc * T + tt * 128:
                              OTALL_O + pc * T + (tt + 1) * 128],
                          wall[:, (pc * 2 + dh) * 512:(pc * 2 + dh + 1) * 512],
                          start=(pc == 0), stop=(pc == 7),
                          inc=(sT_ if pc == 7 else None))
                rv.wait(sT_)
                rv.op("tensor_tensor", sf[:, FFO_O:FFO_O + 512], pb[:],
                      scrA[:, dh * 512:(dh + 1) * 512], ALU.add)
                rv.op("tensor_tensor",
                      x_sb[:, tt * D + dh * 512: tt * D + dh * 512 + 512],
                      x_sb[:, tt * D + dh * 512: tt * D + dh * 512 + 512],
                      sf[:, FFO_O:FFO_O + 512], ALU.add, inc=sV)
        dbg_dump("x1", x_sb[:, 0:dbg_cols])
        if upto == "oproj":
            finish()
            return nc

        rc.wait(sT_)
        layernorm_tiles(make_at_writer(), do_quant=False)
        dbg_dump("bt", at_sb[:, 0:dbg_cols])
        if upto == "ln2":
            finish()
            return nc

        def PW(s):
            return sf[:, PW_O + s * 512: PW_O + (s + 1) * 512]
        H256, T3, FLM, FF, CH, FCH, F8, E1, E2, ACC = range(10)

        rv.wait(sTR)
        rs.wait(sV)
        rs.dma(scrB[:], b2_d.broadcast_to([128, D]), sLD)
        rv.wait(sLD)
        rv.op("tensor_scalar", scrB[:], scrB[:], 256.0, None, ALU.mult, inc=sV)
        first_ffn = True
        st_marks = []
        for half in range(2):
            tbase = half * 1024
            for fc in range(32):
                prep_qkv_style(w1_d[fc], fc % 2, quant=False)
                stat_mark = sV.n
                for tq2 in range(2):
                    pb = psum[tq2]
                    for dc in range(8):
                        if dc == 0:
                            rt.wait(sV, stat_mark)
                            rt.wait(sTR)
                            if not first_ffn:
                                rt.wait(sV)
                        rt.op("matmul", pb[:],
                              wst[:, (fc % 2) * 1024 + dc * 128:
                                  (fc % 2) * 1024 + (dc + 1) * 128],
                              at_sb[:, dc * T + tbase + tq2 * 512:
                                    dc * T + tbase + (tq2 + 1) * 512],
                              start=(dc == 0), stop=(dc == 7),
                              inc=(sT_ if dc == 7 else None))
                    first_ffn = False
                    if tq2 == 1:
                        wstate["wst%d" % (fc % 2)] = sT_.n
                    rv.wait(sT_)
                    rv.op("tensor_scalar", PW(E2), pb[:],
                          b1c_sb[:, fc:fc + 1], None, ALU.add, inc=sV)
                    rc.wait(sV)
                    rc.op("activation", PW(H256), PW(E2), AF.Copy, scale=256.0,
                          bias=MAGIC, inc=sS)
                    rv.wait(sS)
                    rv.op("tensor_scalar", PW(H256), PW(H256), MAGIC, None,
                          ALU.subtract)
                    rv.op("tensor_scalar", PW(T3), PW(H256), 1.0 / 64.0,
                          15.515625, ALU.mult, ALU.add)
                    rv.op("tensor_scalar", PW(FLM), PW(T3), MAGIC, None,
                          ALU.add)
                    rv.op("tensor_scalar", PW(FLM), PW(FLM), MAGIC + 16.0,
                          None, ALU.subtract)
                    rv.op("scalar_tensor_tensor", PW(FF), PW(T3), 15.515625,
                          PW(FLM), ALU.subtract, ALU.subtract)
                    rv.op("tensor_scalar", PW(CH), PW(FLM), 0.125, None,
                          ALU.mult)
                    rv.op("tensor_scalar", PW(ACC), PW(FLM), 0.125, 0.125,
                          ALU.mult, ALU.add, inc=sV)
                    rc.wait(sV)
                    rc.op("activation", PW(E1), PW(CH), AF.Erf, scale=SQRT2)
                    rc.op("activation", PW(E2), PW(ACC), AF.Erf, scale=SQRT2,
                          inc=sS)
                    rg.wait(sV)
                    rg.op("tensor_tensor", PW(FCH), PW(FF), PW(CH), ALU.mult,
                          inc=sG)
                    rv.op("tensor_scalar", PW(F8), PW(FF), 0.125, None,
                          ALU.mult)
                    rv.wait(sG)
                    rv.wait(sS)
                    rv.op("tensor_tensor", PW(ACC), PW(CH), PW(FCH),
                          ALU.subtract, inc=sV)
                    rg.wait(sV)
                    rg.op("tensor_tensor", PW(FCH), PW(FCH), PW(F8), ALU.add,
                          inc=sG)
                    rv.op("tensor_tensor", PW(ACC), PW(ACC), PW(E1), ALU.mult)
                    rv.wait(sG)
                    rv.op("tensor_tensor", PW(E2), PW(FCH), PW(E2), ALU.mult)
                    rv.op("tensor_tensor", PW(ACC), PW(ACC), PW(E2), ALU.add)
                    rv.op("scalar_tensor_tensor", PW(ACC), PW(H256), 2.0 ** -9,
                          PW(ACC), ALU.mult, ALU.add, inc=sV)
                    rc.wait(sV)
                    rc.op("activation", PW(E1), PW(ACC), AF.Copy, scale=256.0,
                          bias=MAGIC, inc=sS)
                    rv.wait(sS)
                    rv.op("tensor_scalar",
                          sbf[:, YQ_O + fc * 1024 + tq2 * 512:
                              YQ_O + fc * 1024 + (tq2 + 1) * 512],
                          PW(E1), MAGIC, 2.0 ** -8, ALU.subtract, ALU.mult,
                          inc=sV)
            yq_mark = sV.n
            if half == 0:
                dbg_dump("yq", sbf[:, YQ_O:YQ_O + dbg_cols])
                if upto == "ffn1":
                    finish()
                    return nc
            for dh in range(2):
                ev_mark = sV.n
                w2marks = []
                for fcw in range(32):
                    wslot = wf32[:, 1024:1536]
                    rs.wait(sV, wstate["wf32b"])
                    rs.dma(wslot, w2_d[fcw, dh], sLD)
                    rv.wait(sLD)
                    if fcw >= 2:
                        rv.wait(sT_, w2marks[fcw - 2])
                    rv.op("tensor_copy",
                          wst[:, (fcw % 2) * 512:(fcw % 2) * 512 + 512],
                          wslot, inc=sV)
                    wstate["wf32b"] = sV.n
                    wmark = sV.n
                    for tt8 in range(8):
                        if tt8 == 0:
                            rt.wait(sV, wmark)
                            rt.wait(sV, yq_mark)
                            if fcw == 0:
                                rt.wait(sV, ev_mark)
                        rt.op("matmul", psum[tt8][:],
                              sbf[:, YQ_O + fcw * 1024 + tt8 * 128:
                                  YQ_O + fcw * 1024 + (tt8 + 1) * 128],
                              wst[:, (fcw % 2) * 512:(fcw % 2) * 512 + 512],
                              start=(fcw == 0), stop=(fcw == 31),
                              skip_group_check=True,
                              inc=(sT_ if tt8 == 7 else None))
                    w2marks.append(sT_.n)
                rv.wait(sT_)
                for tt8 in range(8):
                    gt = half * 8 + tt8
                    s_u = PW(tt8 % 2)
                    s_o = PW(2 + tt8 % 2)
                    if len(st_marks) >= 2:
                        rv.wait(sST, st_marks[-2])
                    rv.op("scalar_tensor_tensor", s_u, psum[tt8][:], 256.0,
                          scrB[:, dh * 512:(dh + 1) * 512], ALU.mult, ALU.add)
                    rv.op("tensor_scalar", s_u, s_u, MAGIC, None, ALU.add)
                    rv.op("tensor_scalar", s_u, s_u, MAGIC, 2.0 ** -8,
                          ALU.subtract, ALU.mult)
                    rv.op("tensor_tensor", s_o, s_u,
                          x_sb[:, gt * D + dh * 512: gt * D + dh * 512 + 512],
                          ALU.add, inc=sV)
                    rs.wait(sV)
                    rs.dma(out_d[gt][:, dh * 512:(dh + 1) * 512], s_o, sST)
                    st_marks.append(sST.n)

        finish()

    return nc


_NC_CACHE = {}


def _get_nc():
    if "nc" not in _NC_CACHE:
        _NC_CACHE["nc"] = build()
    return _NC_CACHE["nc"]


def _layout(L0_attn_x, L0_attn_qkv_kernel, L0_attn_qkv_bias, L0_attn_out_kernel,
            L0_attn_out_bias, L0_ffn_w1, L0_ffn_b1, L0_ffn_w2, L0_ffn_b2):
    x = np.ascontiguousarray(np.asarray(L0_attn_x, dtype=np.float32))
    wq = np.ascontiguousarray(
        np.asarray(L0_attn_qkv_kernel, np.float32)
        .reshape(8, 128, 24, 128).transpose(2, 0, 1, 3))
    bq = np.ascontiguousarray(np.asarray(L0_attn_qkv_bias, np.float32).reshape(24, 128).T)
    wo = np.ascontiguousarray(
        np.asarray(L0_attn_out_kernel, np.float32)
        .reshape(8, 128, 2, 512).transpose(0, 2, 1, 3))
    bo = np.ascontiguousarray(np.asarray(L0_attn_out_bias, np.float32).reshape(1, 1024))
    w1 = np.ascontiguousarray(
        np.asarray(L0_ffn_w1, np.float32)
        .reshape(8, 128, 32, 128).transpose(2, 0, 1, 3))
    b1 = np.ascontiguousarray(np.asarray(L0_ffn_b1, np.float32).reshape(32, 128).T)
    w2 = np.ascontiguousarray(
        np.asarray(L0_ffn_w2, np.float32)
        .reshape(32, 128, 2, 512).transpose(0, 2, 1, 3))
    b2 = np.ascontiguousarray(np.asarray(L0_ffn_b2, np.float32).reshape(1, 1024))
    in_maps = []
    for c in range(8):
        in_maps.append(dict(
            x=np.ascontiguousarray(x[c].reshape(NT, 128, D)),
            wqkv=wq, bqkv=bq, wout=wo, bout=bo,
            w1=w1, b1=b1, w2=w2, b2=b2))
    return in_maps


def kernel(**inputs):
    in_maps = _layout(**inputs)
    nc = _get_nc()
    res = run_bass_kernel_spmd(nc, in_maps, core_ids=list(range(8)))
    out = np.stack([np.asarray(r["out"], np.float32).reshape(T, D)
                    for r in res.results])
    return out


# version 04
# speedup vs baseline: 1.1192x; 1.1192x over previous
import contextlib
import numpy as np
import concourse.bass as bass
import concourse.mybir as mybir
from concourse.bass_utils import run_bass_kernel_spmd

F32 = mybir.dt.float32
BF16 = mybir.dt.bfloat16
AF = mybir.ActivationFunctionType
ALU = mybir.AluOpType
AX = mybir.AxisListType

MAGIC = 12582912.0
VELT = 1048577.0
SQRT2 = float(np.float32(np.sqrt(np.float64(2.0))))
Q_SQRT2 = float(np.float32(0.25 / np.sqrt(np.float64(2.0))))
T = 2048
D = 1024
NT = 16
NPAIR = 8
QT = 4
INV448 = 1.0 / 448.0
EPS_S = 1e-12
LN_EPS = 1e-5


class Sem:
    __slots__ = ("h", "n")

    def __init__(self, h):
        self.h = h
        self.n = 0


class Rec:
    __slots__ = ("ops",)

    def __init__(self):
        self.ops = []

    def wait(self, sem, val=None):
        v = sem.n if val is None else val
        self.ops.append(lambda e, h=sem.h, v=v: e.wait_ge(h, v))

    def op(self, method, *args, inc=None, k=1, **kw):
        if inc is not None:
            inc.n += k

        def f(e, m=method, a=args, kw=kw, inc=inc, k=k):
            r = getattr(e, m)(*a, **kw)
            if inc is not None:
                r.then_inc(inc.h, k)
        self.ops.append(f)

    def dma(self, out, in_, inc, transpose=False):
        inc.n += 16

        def f(e, out=out, in_=in_, inc=inc, tr=transpose):
            if tr:
                e.dma_start(out=out, in_=in_, transpose=True).then_inc(inc.h, 16)
            else:
                e.dma_start(out=out, in_=in_).then_inc(inc.h, 16)
        self.ops.append(f)


def build(dbg_tag=None, dbg_cols=2048, dbg_bf16=False, upto=None):
    nc = bass.Bass("TRN2", target_bir_lowering=False)

    x_d = nc.dram_tensor("x", [NT, 128, D], F32, kind="ExternalInput").ap()
    wq_d = nc.dram_tensor("wqkv", [24, 8, 128, 128], F32, kind="ExternalInput").ap()
    bq_d = nc.dram_tensor("bqkv", [128, 24], F32, kind="ExternalInput").ap()
    wo_d = nc.dram_tensor("wout", [8, 2, 128, 512], F32, kind="ExternalInput").ap()
    bo_d = nc.dram_tensor("bout", [1, D], F32, kind="ExternalInput").ap()
    w1_d = nc.dram_tensor("w1", [32, 8, 128, 128], F32, kind="ExternalInput").ap()
    b1_d = nc.dram_tensor("b1", [128, 32], F32, kind="ExternalInput").ap()
    w2_d = nc.dram_tensor("w2", [32, 2, 128, 512], F32, kind="ExternalInput").ap()
    b2_d = nc.dram_tensor("b2", [1, D], F32, kind="ExternalInput").ap()
    id_d = nc.inline_tensor(np.eye(128, dtype=np.float32), name="ident").ap()
    out_d = nc.dram_tensor("out", [NT, 128, D], F32, kind="ExternalOutput").ap()
    dbg_d = None
    if dbg_tag is not None:
        dbg_d = nc.dram_tensor("dbg", [128, dbg_cols],
                               BF16 if dbg_bf16 else F32,
                               kind="ExternalOutput").ap()

    es = contextlib.ExitStack()
    with es:
        def sb(name, shape, dt):
            return es.enter_context(nc.sbuf_tensor(name, shape, dt))

        x_sb = sb("x_sb", [128, NT * D], F32)
        at_sb = sb("at_sb", [128, 8 * T], BF16)
        sbf = sb("sbf", [128, 32768], BF16)
        sf = sb("sf", [128, 5120], F32)
        scrA = sb("scrA", [128, D], F32)
        scrB = sb("scrB", [128, D], F32)
        scrC = sb("scrC", [128, D], F32)
        scrD = sb("scrD", [128, D], F32)
        stats = sb("stats", [128, 64], F32)
        wst = sb("wst", [128, 2 * 1024], BF16)
        wf32 = sb("wf32", [128, 1536], F32)
        bq_sb = sb("bq_sb", [128, 24], F32)
        b1c_sb = sb("b1c_sb", [128, 32], F32)
        id_sb = sb("id_sb", [128, 128], F32)
        on1_sb = sb("on1_sb", [1, 128], BF16)
        onesA = sb("onesA", [128, 1], F32)
        ngc = sb("ngc", [128, 1], F32)
        half1 = sb("half1", [1, 1], BF16)

        QP_O, KP_O = 0, 2048
        VTA_O, VTB_O = 4096, 5632
        AT_A, AT_B = 7168, 8192
        OTQ_O = 9216
        OTALL_O = 11264
        ATOK_O = 27648
        WOUT_O = 0
        YQ_O = 0

        OTS_O = 0
        OTOK_O = 1024
        OPK_O = 2112
        FFO_O = 4160
        QSQ_O = 4608
        PW_O = 0

        psum = [es.enter_context(nc.psum_tensor(f"ps{i}", [128, 512], F32))
                for i in range(8)]

        S = {n: Sem(es.enter_context(nc.semaphore(n)))
             for n in ["sLD", "sTR", "sST", "sV", "sS", "sT", "sG"]}
        sLD, sTR, sST = S["sLD"], S["sTR"], S["sST"]
        sV, sS, sT_, sG = S["sV"], S["sS"], S["sT"], S["sG"]

        block = es.enter_context(nc.Block())
        rs, rv, rc, rt, rg = Rec(), Rec(), Rec(), Rec(), Rec()

        def finish():
            rs.wait(sST)
            for name, rec in (("sync", rs), ("vector", rv), ("scalar", rc),
                              ("tensor", rt), ("gpsimd", rg)):
                getattr(block, name)(lambda e, ops=rec.ops: [f(e) for f in ops])

        def dbg_dump(tag, src):
            if dbg_tag is None or dbg_tag != tag:
                return
            rs.wait(sV)
            rs.wait(sS)
            rs.wait(sT_)
            rs.wait(sG)
            rs.wait(sTR)
            rs.wait(sST)
            rs.dma(dbg_d[:], src, sST)

        for i in range(NT):
            rs.dma(x_sb[:, i * D:(i + 1) * D], x_d[i], sLD)
        rs.dma(bq_sb[:], bq_d[:], sLD)
        rs.dma(b1c_sb[:], b1_d[:], sLD)
        rs.dma(id_sb[:], id_d[:], sLD)

        rv.wait(sLD)
        rv.op("memset", on1_sb[:], 1.0)
        rv.op("memset", onesA[:], 1.0)
        rv.op("engine_nop", inc=sV)

        if upto == "p0":
            dbg_dump(dbg_tag, x_sb[:, 0:dbg_cols])
            finish()
            return nc

        def layernorm_tiles(writer, do_quant):
            for i in range(NT):
                xt = x_sb[:, i * D:(i + 1) * D]
                rv.op("reduce_sum", stats[:, 0:1], xt, AX.X, inc=sV)
                rv.wait(sV)
                rv.op("tensor_scalar", stats[:, 1:2], stats[:, 0:1],
                      1.0 / D, None, ALU.mult, inc=sV)
                rv.wait(sV)
                rv.op("tensor_scalar", scrA[:], xt, stats[:, 1:2], None,
                      ALU.subtract, inc=sV)
                rc.wait(sV)
                rc.op("activation", scrB[:], scrA[:], AF.Square, scale=1.0,
                      inc=sS)
                rv.wait(sS)
                rv.op("reduce_sum", stats[:, 2:3], scrB[:], AX.X, inc=sV)
                rv.wait(sV)
                rv.op("tensor_scalar", stats[:, 3:4], stats[:, 2:3],
                      1.0 / D, LN_EPS, ALU.mult, ALU.add, inc=sV)
                rc.wait(sV)
                rc.op("activation", stats[:, 4:5], stats[:, 3:4], AF.Sqrt,
                      scale=1.0, inc=sS)
                rv.wait(sS)
                rv.op("reciprocal", stats[:, 5:6], stats[:, 4:5], inc=sV)
                rv.wait(sV)
                rv.op("tensor_scalar", scrB[:], scrA[:], stats[:, 5:6], None,
                      ALU.mult)
                if do_quant:
                    rv.op("tensor_reduce", stats[:, 8:16],
                          scrB[:].rearrange("p (s n) -> p s n", s=8),
                          AX.X, ALU.max, apply_absolute_value=True, inc=sV)
                    rv.wait(sV)
                    rv.op("tensor_scalar", stats[:, 16:24], stats[:, 8:16],
                          INV448, EPS_S, ALU.mult, ALU.max, inc=sV)
                    rv.wait(sV)
                    rv.op("reciprocal", stats[:, 24:32], stats[:, 16:24], inc=sV)
                    rv.wait(sV)
                    for sg in range(8):
                        rv.op("tensor_scalar",
                              scrA[:, sg * 128:(sg + 1) * 128],
                              scrB[:, sg * 128:(sg + 1) * 128],
                              stats[:, 24 + sg:25 + sg], None, ALU.mult)
                    rv.op("tensor_scalar", scrD[:], scrA[:], VELT, None,
                          ALU.mult)
                    rv.op("tensor_tensor", scrC[:], scrD[:], scrA[:],
                          ALU.subtract)
                    rv.op("tensor_tensor", scrC[:], scrD[:], scrC[:],
                          ALU.subtract)
                    for sg in range(8):
                        rv.op("tensor_scalar",
                              scrB[:, sg * 128:(sg + 1) * 128],
                              scrC[:, sg * 128:(sg + 1) * 128],
                              stats[:, 16 + sg:17 + sg], None, ALU.mult)
                writer(i, scrB)

        def make_at_writer():
            marks = []

            def writer(i, src):
                slot = ATOK_O + (i % 2) * 1024
                if len(marks) >= 2:
                    rv.wait(sTR, marks[-2])
                rv.op("tensor_copy", sbf[:, slot:slot + D], src[:], inc=sV)
                rc.wait(sV)
                for ch in range(8):
                    rc.dma(at_sb[:, ch * T + i * 128: ch * T + (i + 1) * 128],
                           sbf[:, slot + ch * 128: slot + (ch + 1) * 128],
                           sTR, transpose=True)
                marks.append(sTR.n)
            return writer

        layernorm_tiles(make_at_writer(), do_quant=True)
        dbg_dump("at", at_sb[:, 0:dbg_cols])
        if upto == "ln1":
            finish()
            return nc

        wstate = {"wf32a": 0, "wf32b": 0, "wst0": 0, "wst1": 0}

        def prep_qkv_style(dram_ap, buf, quant):
            rs.wait(sV, wstate["wf32a"])
            rs.dma(wf32[:, 0:1024].rearrange("p (dc f) -> p dc f", dc=8),
                   dram_ap.rearrange("dc p f -> p dc f"), sLD)
            rv.wait(sLD)
            rv.wait(sT_, wstate["wst%d" % buf])
            if not quant:
                rv.op("tensor_copy", wst[:, buf * 1024:(buf + 1) * 1024],
                      wf32[:, 0:1024], inc=sV)
                wstate["wf32a"] = sV.n
                return
            for dc in range(8):
                w = wf32[:, dc * 128:(dc + 1) * 128]
                rv.op("tensor_reduce", stats[:, 32:33], w, AX.X, ALU.max,
                      apply_absolute_value=True, inc=sV)
                rv.wait(sV)
                rv.op("tensor_scalar", stats[:, 33:34], stats[:, 32:33],
                      INV448, EPS_S, ALU.mult, ALU.max, inc=sV)
                rv.wait(sV)
                rv.op("reciprocal", stats[:, 34:35], stats[:, 33:34], inc=sV)
                rv.wait(sV)
                rv.op("tensor_scalar", scrD[:, 0:128], w, stats[:, 34:35],
                      None, ALU.mult)
                rv.op("tensor_scalar", scrD[:, 128:256], scrD[:, 0:128],
                      VELT, None, ALU.mult)
                rv.op("tensor_tensor", scrD[:, 256:384], scrD[:, 128:256],
                      scrD[:, 0:128], ALU.subtract)
                rv.op("tensor_tensor", scrD[:, 256:384], scrD[:, 128:256],
                      scrD[:, 256:384], ALU.subtract)
                rv.op("tensor_scalar",
                      wst[:, buf * 1024 + dc * 128: buf * 1024 + (dc + 1) * 128],
                      scrD[:, 256:384], stats[:, 33:34], None, ALU.mult,
                      inc=sV)
            wstate["wf32a"] = sV.n

        def quant_mv_chunk(w, dst):
            rv.op("tensor_reduce", stats[:, 36:40],
                  w.rearrange("p (s n) -> p s n", s=4), AX.X, ALU.max,
                  apply_absolute_value=True, inc=sV)
            rv.wait(sV)
            rv.op("tensor_scalar", stats[:, 40:44], stats[:, 36:40],
                  INV448, EPS_S, ALU.mult, ALU.max, inc=sV)
            rv.wait(sV)
            rv.op("reciprocal", stats[:, 44:48], stats[:, 40:44], inc=sV)
            rv.wait(sV)
            for sg in range(4):
                rv.op("tensor_scalar", scrD[:, sg * 128:(sg + 1) * 128],
                      w[:, sg * 128:(sg + 1) * 128],
                      stats[:, 44 + sg:45 + sg], None, ALU.mult)
            rv.op("tensor_scalar", scrD[:, 512:1024], scrD[:, 0:512],
                  VELT, None, ALU.mult)
            rv.op("tensor_tensor", scrC[:, 0:512], scrD[:, 512:1024],
                  scrD[:, 0:512], ALU.subtract)
            rv.op("tensor_tensor", scrC[:, 0:512], scrD[:, 512:1024],
                  scrC[:, 0:512], ALU.subtract)
            for sg in range(4):
                rv.op("tensor_scalar",
                      dst[:, sg * 128:(sg + 1) * 128],
                      scrC[:, sg * 128:(sg + 1) * 128],
                      stats[:, 40 + sg:41 + sg], None, ALU.mult,
                      inc=(sV if sg == 3 else None))

        at_ready = sTR.n
        first_qkv = True
        for p in range(NPAIR):
            vtr_marks = []
            for which, oc in ((0, p), (1, 8 + p), (2, 16 + p)):
                dst_off = (QP_O, KP_O, None)[which]
                prep_qkv_style(wq_d[oc], oc % 2, quant=True)
                stat_mark = sV.n
                for q4 in range(QT):
                    pb = psum[q4 % 2]
                    for dc in range(8):
                        if dc == 0:
                            rt.wait(sV, stat_mark)
                            rt.wait(sV)
                            if first_qkv:
                                rt.wait(sTR, at_ready)
                                first_qkv = False
                        rt.op("matmul", pb[:],
                              wst[:, (oc % 2) * 1024 + dc * 128:
                                  (oc % 2) * 1024 + (dc + 1) * 128],
                              at_sb[:, dc * T + q4 * 512: dc * T + (q4 + 1) * 512],
                              start=(dc == 0), stop=(dc == 7),
                              inc=(sT_ if dc == 7 else None))
                    if q4 == QT - 1:
                        wstate["wst%d" % (oc % 2)] = sT_.n
                    rv.wait(sT_)
                    if which == 2:
                        stg = AT_A + (q4 % 2) * 512
                        if len(vtr_marks) >= 2:
                            rv.wait(sTR, vtr_marks[-2])
                        rv.op("tensor_scalar", sbf[:, stg:stg + 512], pb[:],
                              bq_sb[:, oc:oc + 1], None, ALU.add, inc=sV)
                        rs.wait(sV)
                        for j in range(4):
                            kt = q4 * 4 + j
                            vsrc = sbf[:, stg + j * 128: stg + (j + 1) * 128]
                            rs.dma(sbf[:, VTA_O + kt * 96: VTA_O + kt * 96 + 64],
                                   vsrc[0:64, :], sTR, transpose=True)
                            rs.dma(sbf[:, VTB_O + kt * 96: VTB_O + kt * 96 + 64],
                                   vsrc[64:128, :], sTR, transpose=True)
                        vtr_marks.append(sTR.n)
                    else:
                        rv.op("tensor_scalar",
                              sbf[:, dst_off + q4 * 512: dst_off + (q4 + 1) * 512],
                              pb[:], bq_sb[:, oc:oc + 1], None, ALU.add,
                              inc=sV)
            vtr_mark = sTR.n
            rv.wait(sTR, vtr_mark)
            for kt in range(NT):
                rv.op("memset",
                      sbf[:, VTA_O + kt * 96 + 64: VTA_O + kt * 96 + 65], 1.0)
                rv.op("memset",
                      sbf[:, VTB_O + kt * 96 + 64: VTB_O + kt * 96 + 65], 1.0,
                      inc=(sV if kt == NT - 1 else None))
            dbg_dump(f"qkvq{p}", sbf[:, QP_O:QP_O + min(dbg_cols, 2048)])
            dbg_dump(f"qkvv{p}", sbf[:, VTA_O:VTA_O + min(dbg_cols, 1536)])
            if upto == "qkv0" and p == 0:
                finish()
                return nc

            for idx, src_off in ((0, QP_O), (1, KP_O)):
                for q4 in range(QT):
                    bank = 6 + idx
                    rc.wait(sT_)
                    rc.wait(sV)
                    rc.op("activation", sf[:, QSQ_O:QSQ_O + 512],
                          sbf[:, src_off + q4 * 512: src_off + (q4 + 1) * 512],
                          AF.Square, scale=1.0, inc=sS)
                    rt.wait(sS)
                    for hh in range(2):
                        rt.op("matmul", psum[bank][hh * 64:hh * 64 + 1, :],
                              onesA[hh * 64:(hh + 1) * 64, :],
                              sf[hh * 64:(hh + 1) * 64, QSQ_O:QSQ_O + 512],
                              start=(q4 == 0), stop=(q4 == QT - 1),
                              tile_position=(hh * 64, hh * 64),
                              skip_group_check=True,
                              inc=(sT_ if hh == 1 else None))
            rv.wait(sT_)
            for bi, bank in enumerate((6, 7)):
                for row in (0, 64):
                    rv.op("tensor_reduce", stats[row:row + 1, 48 + bi:49 + bi],
                          psum[bank][row:row + 1, :], AX.X, ALU.max, inc=sV)
                    rv.wait(sV)
            rt.wait(sV)
            rt.op("transpose", psum[2][0:1, 0:128], stats[:, 48:49], id_sb[:],
                  inc=sT_)
            rt.op("transpose", psum[3][0:1, 0:128], stats[:, 49:50], id_sb[:],
                  inc=sT_)
            rv.wait(sT_)
            rv.op("tensor_copy", sf[0:1, 0:128], psum[2][0:1, 0:128])
            rv.op("tensor_copy", sf[0:1, 128:256], psum[3][0:1, 0:128], inc=sV)
            rv.wait(sV)
            rv.op("tensor_tensor", sf[0:1, 240:241], sf[0:1, 0:1],
                  sf[0:1, 64:65], ALU.max)
            rv.op("tensor_tensor", sf[0:1, 242:243], sf[0:1, 128:129],
                  sf[0:1, 192:193], ALU.max, inc=sV)
            rv.wait(sV)
            rv.op("tensor_tensor", sf[0:1, 244:245], sf[0:1, 240:241],
                  sf[0:1, 242:243], ALU.mult, inc=sV)
            rc.wait(sV)
            rc.op("activation", sf[0:1, 245:246], sf[0:1, 244:245],
                  AF.Sqrt, scale=1.0, inc=sS)
            rv.wait(sS)
            rv.op("tensor_scalar", sf[0:1, 245:246], sf[0:1, 245:246],
                  0.125, 80.0, ALU.mult, ALU.min, inc=sV)
            rv.wait(sV)
            rv.op("tensor_scalar", half1[:], sf[0:1, 245:246], -1.0, None,
                  ALU.mult, inc=sV)
            rt.wait(sV)
            rt.op("matmul", psum[6][:, 8:9], on1_sb[:], half1[:],
                  start=True, stop=True, inc=sT_)
            rv.wait(sT_)
            rv.op("tensor_copy", ngc[:], psum[6][:, 8:9], inc=sV)
            ngc_mark = sV.n
            if upto == "bound0" and p == 0:
                dbg_dump("c1row", sf[:, 0:256])
                dbg_dump("ngcd", ngc[:])
                dbg_dump("nrm", stats[:, 48:50])
                finish()
                return nc

            for q4 in range(QT):
                av_free_mark = sV.n
                exp_marks = []

                def emit_scores(kt):
                    sbA = psum[2 + (kt % 2)]
                    sbB = psum[6 + (kt % 2)]
                    if kt >= 2:
                        rt.wait(sS, exp_marks[kt - 2])
                    else:
                        rt.wait(sV, av_free_mark)
                    rt.op("matmul", sbA[:],
                          sbf[0:64, KP_O + kt * 128: KP_O + (kt + 1) * 128],
                          sbf[0:64, QP_O + q4 * 512: QP_O + (q4 + 1) * 512],
                          start=True, stop=True, tile_position=(0, 0))
                    rt.op("matmul", sbB[:],
                          sbf[64:128, KP_O + kt * 128: KP_O + (kt + 1) * 128],
                          sbf[64:128, QP_O + q4 * 512: QP_O + (q4 + 1) * 512],
                          start=True, stop=True, tile_position=(64, 0),
                          inc=sT_)

                def emit_exp(kt):
                    sbA = psum[2 + (kt % 2)]
                    sbB = psum[6 + (kt % 2)]
                    rc.wait(sT_)
                    if kt == 0:
                        rc.wait(sV, ngc_mark)
                        rc.wait(sTR, vtr_mark)
                    rc.op("activation",
                          sbf[:, AT_A + (kt % 2) * 512: AT_A + (kt % 2) * 512 + 512],
                          sbA[:], AF.Exp, scale=0.125, bias=ngc[:])
                    rc.op("activation",
                          sbf[:, AT_B + (kt % 2) * 512: AT_B + (kt % 2) * 512 + 512],
                          sbB[:], AF.Exp, scale=0.125, bias=ngc[:], inc=sS)
                    exp_marks.append(sS.n)

                def emit_av(kt):
                    rt.wait(sS, exp_marks[kt])
                    rt.op("matmul", psum[4][0:65, :],
                          sbf[:, VTA_O + kt * 96: VTA_O + kt * 96 + 65],
                          sbf[:, AT_A + (kt % 2) * 512: AT_A + (kt % 2) * 512 + 512],
                          start=(kt == 0), stop=(kt == NT - 1),
                          skip_group_check=True)
                    rt.op("matmul", psum[5][0:65, :],
                          sbf[:, VTB_O + kt * 96: VTB_O + kt * 96 + 65],
                          sbf[:, AT_B + (kt % 2) * 512: AT_B + (kt % 2) * 512 + 512],
                          start=(kt == 0), stop=(kt == NT - 1),
                          skip_group_check=True,
                          inc=(sT_ if kt == NT - 1 else None))

                emit_scores(0)
                emit_exp(0)
                for kt in range(1, NT):
                    emit_scores(kt)
                    emit_av(kt - 1)
                    emit_exp(kt)
                emit_av(NT - 1)
                rv.wait(sT_)
                rv.op("tensor_copy", sf[0:65, OTS_O:OTS_O + 512], psum[4][0:65, :])
                rv.op("tensor_copy", sf[0:65, OTS_O + 512:OTS_O + 1024],
                      psum[5][0:65, :], inc=sV)
                if upto == "av0" and p == 0 and q4 == 0:
                    dbg_dump("oraw", sf[:, OTS_O:OTS_O + 1024])
                    finish()
                    return nc
                tr_mark = sV.n
                for hh, ofs in ((0, OTS_O), (1, OTS_O + 512)):
                    for j in range(4):
                        kt = q4 * 4 + j
                        rt.wait(sV)
                        rt.op("transpose", psum[2 + hh][:, 0:65],
                              sf[0:65, ofs + j * 128: ofs + (j + 1) * 128],
                              id_sb[0:65, 0:65], inc=sT_)
                        rv.wait(sT_)
                        rv.op("tensor_copy",
                              sf[:, OTOK_O + kt * 66: OTOK_O + kt * 66 + 65],
                              psum[2 + hh][:, 0:65], inc=sV)
                        rv.wait(sV)
                        rv.op("reciprocal", stats[:, 52:53],
                              sf[:, OTOK_O + kt * 66 + 64: OTOK_O + kt * 66 + 65], inc=sV)
                        rv.wait(sV)
                        rv.op("tensor_scalar",
                              sf[:, OPK_O + kt * 128 + hh * 64:
                                 OPK_O + kt * 128 + hh * 64 + 64],
                              sf[:, OTOK_O + kt * 66: OTOK_O + kt * 66 + 64],
                              stats[:, 52:53], None, ALU.mult, inc=sV)
                if upto == "otr0" and p == 0 and q4 == 0:
                    dbg_dump("opkq", sf[:, OPK_O:OPK_O + 512])
                    dbg_dump("otokq", sf[:, OTOK_O:OTOK_O + 1056])
                    finish()
                    return nc
            dbg_dump(f"opk{p}", sf[:, OPK_O:OPK_O + min(dbg_cols, 2048)])

            opk = sf[:, OPK_O:OPK_O + 2048]
            rv.op("tensor_reduce", stats[:, 8:24],
                  opk.rearrange("p (s n) -> p s n", s=16), AX.X, ALU.max,
                  apply_absolute_value=True, inc=sV)
            rv.wait(sV)
            rv.op("tensor_scalar", stats[:, 24:40], stats[:, 8:24],
                  INV448, EPS_S, ALU.mult, ALU.max, inc=sV)
            rv.wait(sV)
            rv.op("reciprocal", stats[:, 40:56], stats[:, 24:40], inc=sV)
            rv.wait(sV)
            for sg in range(16):
                dsty = (sf[:, OTOK_O + sg * 128: OTOK_O + (sg + 1) * 128]
                        if sg < 8 else
                        sf[:, OTS_O + (sg - 8) * 128: OTS_O + (sg - 7) * 128])
                rv.op("tensor_scalar", dsty, opk[:, sg * 128:(sg + 1) * 128],
                      stats[:, 40 + sg:41 + sg], None, ALU.mult)
            for hb, yo in ((0, OTOK_O), (1, OTS_O)):
                ysl = sf[:, yo:yo + 1024]
                rv.op("tensor_scalar", scrD[:], ysl, VELT, None, ALU.mult)
                rv.op("tensor_tensor", scrC[:], scrD[:], ysl, ALU.subtract)
                rv.op("tensor_tensor", scrC[:], scrD[:], scrC[:], ALU.subtract)
                for sg in range(8):
                    g = hb * 8 + sg
                    rv.op("tensor_scalar",
                          sbf[:, OTQ_O + g * 128: OTQ_O + (g + 1) * 128],
                          scrC[:, sg * 128:(sg + 1) * 128],
                          stats[:, 24 + g:25 + g], None, ALU.mult,
                          inc=(sV if g == 15 else None))
            rs.wait(sV)
            for tt in range(NT):
                rs.dma(sbf[:, OTALL_O + p * T + tt * 128:
                           OTALL_O + p * T + (tt + 1) * 128],
                       sbf[:, OTQ_O + tt * 128: OTQ_O + (tt + 1) * 128],
                       sTR, transpose=True)
            dbg_dump(f"otq{p}", sbf[:, OTQ_O:OTQ_O + min(dbg_cols, 2048)])
            if upto == "pair0" and p == 0:
                finish()
                return nc

        rv.wait(sT_)
        wall = sbf[:, WOUT_O:WOUT_O + 16 * 512]
        for pc in range(8):
            for dh in range(2):
                k = pc * 2 + dh
                wslot = wf32[:, 1024:1536]
                rs.wait(sV, wstate["wf32b"])
                rs.dma(wslot, wo_d[pc, dh], sLD)
                rv.wait(sLD)
                quant_mv_chunk(wslot, wall[:, k * 512:(k + 1) * 512])
                wstate["wf32b"] = sV.n
        rs.wait(sV)
        rs.dma(scrA[:], bo_d.broadcast_to([128, D]), sLD)
        rv.wait(sLD)
        rv.op("engine_nop", inc=sV)
        wout_mark = sV.n
        rt.wait(sTR)
        for tt in range(NT):
            for dh in range(2):
                pb = psum[(tt * 2 + dh) % 2]
                for pc in range(8):
                    if pc == 0:
                        rt.wait(sV, wout_mark)
                        rt.wait(sV)
                    rt.op("matmul", pb[:],
                          sbf[:, OTALL_O + pc * T + tt * 128:
                              OTALL_O + pc * T + (tt + 1) * 128],
                          wall[:, (pc * 2 + dh) * 512:(pc * 2 + dh + 1) * 512],
                          start=(pc == 0), stop=(pc == 7),
                          inc=(sT_ if pc == 7 else None))
                rv.wait(sT_)
                rv.op("tensor_tensor", sf[:, FFO_O:FFO_O + 512], pb[:],
                      scrA[:, dh * 512:(dh + 1) * 512], ALU.add)
                rv.op("tensor_tensor",
                      x_sb[:, tt * D + dh * 512: tt * D + dh * 512 + 512],
                      x_sb[:, tt * D + dh * 512: tt * D + dh * 512 + 512],
                      sf[:, FFO_O:FFO_O + 512], ALU.add, inc=sV)
        dbg_dump("x1", x_sb[:, 0:dbg_cols])
        if upto == "oproj":
            finish()
            return nc

        rc.wait(sT_)
        layernorm_tiles(make_at_writer(), do_quant=False)
        dbg_dump("bt", at_sb[:, 0:dbg_cols])
        if upto == "ln2":
            finish()
            return nc

        def PW(s):
            return sf[:, PW_O + s * 512: PW_O + (s + 1) * 512]
        H256, T3, FLM, FF, CH, FCH, F8, E1, E2, ACC = range(10)

        rv.wait(sTR)
        rs.wait(sV)
        rs.dma(sf[:, 4096:5120], b2_d.broadcast_to([128, D]), sLD)
        rv.wait(sLD)
        rv.op("tensor_scalar", sf[:, 4096:5120], sf[:, 4096:5120], 256.0,
              None, ALU.mult, inc=sV)
        first_ffn = True
        st_marks = []
        for half in range(2):
            tbase = half * 1024
            def slot(b, s):
                if b == 0:
                    return sf[:, s * 512:(s + 1) * 512]
                t = (scrA, scrB, scrC, scrD)[s // 2]
                return t[:, (s % 2) * 512:(s % 2) * 512 + 512]

            endm = {}
            u1m = {}
            castm = {}
            chunks = [(fc, tq2) for fc in range(32) for tq2 in range(2)]

            def emit_mm(n):
                fc, tq2 = chunks[n]
                if tq2 == 0:
                    prep_qkv_style(w1_d[fc], fc % 2, quant=False)
                    castm[fc] = sV.n
                pb = psum[n % 2]
                for dc in range(8):
                    if dc == 0:
                        rt.wait(sV, castm[fc])
                        rt.wait(sTR)
                        if n >= 2:
                            rt.wait(sV, u1m[n - 2])
                        if n < 2:
                            rt.wait(sV)
                    rt.op("matmul", pb[:],
                          wst[:, (fc % 2) * 1024 + dc * 128:
                              (fc % 2) * 1024 + (dc + 1) * 128],
                          at_sb[:, dc * T + tbase + tq2 * 512:
                                dc * T + tbase + (tq2 + 1) * 512],
                          start=(dc == 0), stop=(dc == 7),
                          inc=(sT_ if dc == 7 else None))
                if tq2 == 1:
                    wstate["wst%d" % (fc % 2)] = sT_.n
                return sT_.n

            def emit_chunk(n, mm_mark):
                fc, tq2 = chunks[n]
                b = n % 2
                S_ = lambda s: slot(b, s)
                pb = psum[n % 2]
                rv.wait(sT_, mm_mark)
                if n == 0:
                    rv.wait(sST)
                if n >= 2:
                    ps_, pg = endm[n - 2]
                    rv.wait(sS, ps_)
                    rv.wait(sG, pg)
                rv.op("tensor_scalar", S_(0), pb[:], b1c_sb[:, fc:fc + 1],
                      None, ALU.add, inc=sV)
                u1m[n] = sV.n
                rc.wait(sV, u1m[n])
                rc.op("activation", S_(1), S_(0), AF.Copy, scale=256.0,
                      bias=MAGIC, inc=sS)
                rc.op("activation", S_(2), S_(1), AF.Copy, scale=1.0 / 64.0,
                      bias=-196592.484375, inc=sS)
                m_t3 = sS.n
                rv.wait(sS, m_t3)
                rv.op("tensor_scalar", S_(1), S_(1), MAGIC, None, ALU.subtract)
                rv.op("tensor_scalar", S_(3), S_(2), MAGIC, None, ALU.add)
                rv.op("tensor_scalar", S_(3), S_(3), MAGIC + 16.0, None,
                      ALU.subtract)
                rv.op("scalar_tensor_tensor", S_(4), S_(2), 15.515625, S_(3),
                      ALU.subtract, ALU.subtract)
                rv.op("tensor_scalar", S_(5), S_(3), 0.125, None, ALU.mult,
                      inc=sV)
                m_fe = sV.n
                rc.wait(sV, m_fe)
                rc.op("activation", S_(0), S_(4), AF.Copy, scale=0.125,
                      bias=0.0, inc=sS)
                m_f8 = sS.n
                rc.op("activation", S_(6), S_(3), AF.Copy, scale=0.125,
                      bias=0.125)
                rc.op("activation", S_(2), S_(5), AF.Erf, scale=SQRT2)
                rc.op("activation", S_(3), S_(6), AF.Erf, scale=SQRT2, inc=sS)
                m_e2 = sS.n
                rg.wait(sV, m_fe)
                rg.op("tensor_tensor", S_(7), S_(4), S_(5), ALU.mult, inc=sG)
                m_fch = sG.n
                rg.wait(sS, m_f8)
                rg.op("tensor_tensor", S_(4), S_(7), S_(0), ALU.add, inc=sG)
                m_bh = sG.n
                rv.wait(sG, m_fch)
                rv.wait(sS, m_e2)
                rv.op("tensor_tensor", S_(6), S_(5), S_(7), ALU.subtract)
                rv.op("tensor_tensor", S_(5), S_(6), S_(2), ALU.mult)
                rv.wait(sG, m_bh)
                rv.op("tensor_tensor", S_(7), S_(4), S_(3), ALU.mult)
                rv.op("tensor_tensor", S_(2), S_(5), S_(7), ALU.add)
                rv.op("scalar_tensor_tensor", S_(4), S_(1), 2.0 ** -9, S_(2),
                      ALU.mult, ALU.add, inc=sV)
                m_y = sV.n
                rc.wait(sV, m_y)
                rc.op("activation", S_(1), S_(4), AF.Copy, scale=256.0,
                      bias=MAGIC, inc=sS)
                rc.op("activation",
                      sbf[:, YQ_O + fc * 1024 + tq2 * 512:
                          YQ_O + fc * 1024 + (tq2 + 1) * 512],
                      S_(1), AF.Copy, scale=2.0 ** -8, bias=-49152.0, inc=sS)
                endm[n] = (sS.n, sG.n)

            mm_marks = {0: emit_mm(0)}
            for n in range(len(chunks)):
                if n + 1 < len(chunks):
                    mm_marks[n + 1] = emit_mm(n + 1)
                emit_chunk(n, mm_marks[n])
            yq_mark = sS.n
            if half == 0:
                dbg_dump("yq", sbf[:, YQ_O:YQ_O + dbg_cols])
                if upto == "ffn1":
                    finish()
                    return nc
            for dh in range(2):
                ev_mark = sV.n
                w2marks = []
                for fcw in range(32):
                    wslot = wf32[:, 1024:1536]
                    rs.wait(sV, wstate["wf32b"])
                    rs.dma(wslot, w2_d[fcw, dh], sLD)
                    rv.wait(sLD)
                    if fcw >= 2:
                        rv.wait(sT_, w2marks[fcw - 2])
                    rv.op("tensor_copy",
                          wst[:, (fcw % 2) * 512:(fcw % 2) * 512 + 512],
                          wslot, inc=sV)
                    wstate["wf32b"] = sV.n
                    wmark = sV.n
                    for tt8 in range(8):
                        if tt8 == 0:
                            rt.wait(sV, wmark)
                            rt.wait(sS, yq_mark)
                            if fcw == 0:
                                rt.wait(sV, ev_mark)
                        rt.op("matmul", psum[tt8][:],
                              sbf[:, YQ_O + fcw * 1024 + tt8 * 128:
                                  YQ_O + fcw * 1024 + (tt8 + 1) * 128],
                              wst[:, (fcw % 2) * 512:(fcw % 2) * 512 + 512],
                              start=(fcw == 0), stop=(fcw == 31),
                              skip_group_check=True,
                              inc=(sT_ if tt8 == 7 else None))
                    w2marks.append(sT_.n)
                rv.wait(sT_)
                for tt8 in range(8):
                    gt = half * 8 + tt8
                    s_u = PW(tt8 % 2)
                    s_o = PW(2 + tt8 % 2)
                    if len(st_marks) >= 2:
                        rv.wait(sST, st_marks[-2])
                    rv.op("scalar_tensor_tensor", s_u, psum[tt8][:], 256.0,
                          sf[:, 4096 + dh * 512:4096 + (dh + 1) * 512],
                          ALU.mult, ALU.add)
                    rv.op("tensor_scalar", s_u, s_u, MAGIC, None, ALU.add)
                    rv.op("tensor_scalar", s_u, s_u, MAGIC, 2.0 ** -8,
                          ALU.subtract, ALU.mult)
                    rv.op("tensor_tensor", s_o, s_u,
                          x_sb[:, gt * D + dh * 512: gt * D + dh * 512 + 512],
                          ALU.add, inc=sV)
                    rs.wait(sV)
                    rs.dma(out_d[gt][:, dh * 512:(dh + 1) * 512], s_o, sST)
                    st_marks.append(sST.n)

        finish()

    return nc


_NC_CACHE = {}


def _get_nc():
    if "nc" not in _NC_CACHE:
        _NC_CACHE["nc"] = build()
    return _NC_CACHE["nc"]


def _layout(L0_attn_x, L0_attn_qkv_kernel, L0_attn_qkv_bias, L0_attn_out_kernel,
            L0_attn_out_bias, L0_ffn_w1, L0_ffn_b1, L0_ffn_w2, L0_ffn_b2):
    x = np.ascontiguousarray(np.asarray(L0_attn_x, dtype=np.float32))
    wq = np.ascontiguousarray(
        np.asarray(L0_attn_qkv_kernel, np.float32)
        .reshape(8, 128, 24, 128).transpose(2, 0, 1, 3))
    bq = np.ascontiguousarray(np.asarray(L0_attn_qkv_bias, np.float32).reshape(24, 128).T)
    wo = np.ascontiguousarray(
        np.asarray(L0_attn_out_kernel, np.float32)
        .reshape(8, 128, 2, 512).transpose(0, 2, 1, 3))
    bo = np.ascontiguousarray(np.asarray(L0_attn_out_bias, np.float32).reshape(1, 1024))
    w1 = np.ascontiguousarray(
        np.asarray(L0_ffn_w1, np.float32)
        .reshape(8, 128, 32, 128).transpose(2, 0, 1, 3))
    b1 = np.ascontiguousarray(np.asarray(L0_ffn_b1, np.float32).reshape(32, 128).T)
    w2 = np.ascontiguousarray(
        np.asarray(L0_ffn_w2, np.float32)
        .reshape(32, 128, 2, 512).transpose(0, 2, 1, 3))
    b2 = np.ascontiguousarray(np.asarray(L0_ffn_b2, np.float32).reshape(1, 1024))
    in_maps = []
    for c in range(8):
        in_maps.append(dict(
            x=np.ascontiguousarray(x[c].reshape(NT, 128, D)),
            wqkv=wq, bqkv=bq, wout=wo, bout=bo,
            w1=w1, b1=b1, w2=w2, b2=b2))
    return in_maps


def kernel(**inputs):
    in_maps = _layout(**inputs)
    nc = _get_nc()
    res = run_bass_kernel_spmd(nc, in_maps, core_ids=list(range(8)))
    out = np.stack([np.asarray(r["out"], np.float32).reshape(T, D)
                    for r in res.results])
    return out


# version 05 (final)
# speedup vs baseline: 1.2923x; 1.1546x over previous
import contextlib
import numpy as np
import concourse.bass as bass
import concourse.mybir as mybir
from concourse.bass_utils import run_bass_kernel_spmd

F32 = mybir.dt.float32
BF16 = mybir.dt.bfloat16
AF = mybir.ActivationFunctionType
ALU = mybir.AluOpType
AX = mybir.AxisListType

MAGIC = 12582912.0
VELT = 1048577.0
SQRT2 = float(np.float32(np.sqrt(np.float64(2.0))))
Q_SQRT2 = float(np.float32(0.25 / np.sqrt(np.float64(2.0))))
T = 2048
D = 1024
NT = 16
NPAIR = 8
QT = 4
INV448 = 1.0 / 448.0
EPS_S = 1e-12
LN_EPS = 1e-5


class Sem:
    __slots__ = ("h", "n")

    def __init__(self, h):
        self.h = h
        self.n = 0


class Rec:
    __slots__ = ("ops",)

    def __init__(self):
        self.ops = []

    def wait(self, sem, val=None):
        v = sem.n if val is None else val
        self.ops.append(lambda e, h=sem.h, v=v: e.wait_ge(h, v))

    def op(self, method, *args, inc=None, k=1, **kw):
        if inc is not None:
            inc.n += k

        def f(e, m=method, a=args, kw=kw, inc=inc, k=k):
            r = getattr(e, m)(*a, **kw)
            if inc is not None:
                r.then_inc(inc.h, k)
        self.ops.append(f)

    def dma(self, out, in_, inc, transpose=False):
        inc.n += 16

        def f(e, out=out, in_=in_, inc=inc, tr=transpose):
            if tr:
                e.dma_start(out=out, in_=in_, transpose=True).then_inc(inc.h, 16)
            else:
                e.dma_start(out=out, in_=in_).then_inc(inc.h, 16)
        self.ops.append(f)


def build(dbg_tag=None, dbg_cols=2048, dbg_bf16=False, upto=None):
    nc = bass.Bass("TRN2", target_bir_lowering=False)

    x_d = nc.dram_tensor("x", [NT, 128, D], F32, kind="ExternalInput").ap()
    wq_d = nc.dram_tensor("wqkv", [24, 8, 128, 128], F32, kind="ExternalInput").ap()
    bq_d = nc.dram_tensor("bqkv", [128, 24], F32, kind="ExternalInput").ap()
    wo_d = nc.dram_tensor("wout", [8, 2, 128, 512], F32, kind="ExternalInput").ap()
    bo_d = nc.dram_tensor("bout", [1, D], F32, kind="ExternalInput").ap()
    w1_d = nc.dram_tensor("w1", [32, 8, 128, 128], F32, kind="ExternalInput").ap()
    b1_d = nc.dram_tensor("b1", [128, 32], F32, kind="ExternalInput").ap()
    w2_d = nc.dram_tensor("w2", [32, 2, 128, 512], F32, kind="ExternalInput").ap()
    b2_d = nc.dram_tensor("b2", [1, D], F32, kind="ExternalInput").ap()
    id_d = nc.inline_tensor(np.eye(128, dtype=np.float32), name="ident").ap()
    out_d = nc.dram_tensor("out", [NT, 128, D], F32, kind="ExternalOutput").ap()
    dbg_d = None
    if dbg_tag is not None:
        dbg_d = nc.dram_tensor("dbg", [128, dbg_cols],
                               BF16 if dbg_bf16 else F32,
                               kind="ExternalOutput").ap()

    es = contextlib.ExitStack()
    with es:
        def sb(name, shape, dt):
            return es.enter_context(nc.sbuf_tensor(name, shape, dt))

        x_sb = sb("x_sb", [128, NT * D], F32)
        at_sb = sb("at_sb", [128, 8 * T], BF16)
        sbf = sb("sbf", [128, 32768], BF16)
        sf = sb("sf", [128, 5120], F32)
        scrA = sb("scrA", [128, D], F32)
        scrB = sb("scrB", [128, D], F32)
        scrC = sb("scrC", [128, D], F32)
        scrD = sb("scrD", [128, D], F32)
        stats = sb("stats", [128, 64], F32)
        wst = sb("wst", [128, 2 * 1024], BF16)
        wf32 = sb("wf32", [128, 1536], F32)
        bq_sb = sb("bq_sb", [128, 24], F32)
        b1c_sb = sb("b1c_sb", [128, 32], F32)
        id_sb = sb("id_sb", [128, 128], F32)
        on1_sb = sb("on1_sb", [1, 128], BF16)
        onesA = sb("onesA", [128, 1], F32)
        ngc = sb("ngc", [128, 1], F32)
        ones2 = sb("ones2", [128, 2], F32)
        half1 = sb("half1", [1, 1], BF16)

        QP_O, KP_O = 0, 2048
        VTA_O, VTB_O = 4096, 5632
        AT_A, AT_B = 7168, 8192
        OTQ_O = 9216
        OTALL_O = 11264
        ATOK_O = 27648
        WOUT_O = 0
        YQ_O = 0

        OTS_O = 0
        OTOK_O = 1024
        OPK_O = 2112
        FFO_O = 4160
        QSQ_O = 4608
        PW_O = 0

        psum = [es.enter_context(nc.psum_tensor(f"ps{i}", [128, 512], F32))
                for i in range(8)]

        S = {n: Sem(es.enter_context(nc.semaphore(n)))
             for n in ["sLD", "sLW", "sTR", "sST", "sV", "sS", "sT", "sG"]}
        sLD, sLW, sTR, sST = S["sLD"], S["sLW"], S["sTR"], S["sST"]
        sV, sS, sT_, sG = S["sV"], S["sS"], S["sT"], S["sG"]

        block = es.enter_context(nc.Block())
        rs, rv, rc, rt, rg = Rec(), Rec(), Rec(), Rec(), Rec()

        def finish():
            rs.wait(sST)
            for name, rec in (("sync", rs), ("vector", rv), ("scalar", rc),
                              ("tensor", rt), ("gpsimd", rg)):
                getattr(block, name)(lambda e, ops=rec.ops: [f(e) for f in ops])

        def dbg_dump(tag, src):
            if dbg_tag is None or dbg_tag != tag:
                return
            rs.wait(sV)
            rs.wait(sS)
            rs.wait(sT_)
            rs.wait(sG)
            rs.wait(sTR)
            rs.wait(sST)
            rs.dma(dbg_d[:], src, sST)

        for i in range(NT):
            rs.dma(x_sb[:, i * D:(i + 1) * D], x_d[i], sLD)
        rs.dma(bq_sb[:], bq_d[:], sLD)
        rs.dma(b1c_sb[:], b1_d[:], sLD)
        rs.dma(id_sb[:], id_d[:], sLD)

        rv.wait(sLD)
        rv.op("memset", on1_sb[:], 1.0)
        rv.op("memset", onesA[:], 1.0)
        rv.op("memset", ones2[0:64, 0:1], 1.0)
        rv.op("memset", ones2[64:128, 0:1], 0.0)
        rv.op("memset", ones2[0:64, 1:2], 0.0)
        rv.op("memset", ones2[64:128, 1:2], 1.0)
        rv.op("engine_nop", inc=sV)

        if upto == "p0":
            dbg_dump(dbg_tag, x_sb[:, 0:dbg_cols])
            finish()
            return nc

        def layernorm_tiles(writer, do_quant):
            for i in range(NT):
                xt = x_sb[:, i * D:(i + 1) * D]
                rv.op("reduce_sum", stats[:, 0:1], xt, AX.X, inc=sV)
                rv.wait(sV)
                rv.op("tensor_scalar", stats[:, 1:2], stats[:, 0:1],
                      1.0 / D, None, ALU.mult, inc=sV)
                rv.wait(sV)
                rv.op("tensor_scalar", scrA[:], xt, stats[:, 1:2], None,
                      ALU.subtract, inc=sV)
                rc.wait(sV)
                rc.op("activation", scrB[:], scrA[:], AF.Square, scale=1.0,
                      inc=sS)
                rv.wait(sS)
                rv.op("reduce_sum", stats[:, 2:3], scrB[:], AX.X, inc=sV)
                rv.wait(sV)
                rv.op("tensor_scalar", stats[:, 3:4], stats[:, 2:3],
                      1.0 / D, LN_EPS, ALU.mult, ALU.add, inc=sV)
                rc.wait(sV)
                rc.op("activation", stats[:, 4:5], stats[:, 3:4], AF.Sqrt,
                      scale=1.0, inc=sS)
                rv.wait(sS)
                rv.op("reciprocal", stats[:, 5:6], stats[:, 4:5], inc=sV)
                rv.wait(sV)
                rv.op("tensor_scalar", scrB[:], scrA[:], stats[:, 5:6], None,
                      ALU.mult)
                if do_quant:
                    rv.op("tensor_reduce", stats[:, 8:16],
                          scrB[:].rearrange("p (s n) -> p s n", s=8),
                          AX.X, ALU.max, apply_absolute_value=True, inc=sV)
                    rv.wait(sV)
                    rv.op("tensor_scalar", stats[:, 16:24], stats[:, 8:16],
                          INV448, EPS_S, ALU.mult, ALU.max, inc=sV)
                    rv.wait(sV)
                    rv.op("reciprocal", stats[:, 24:32], stats[:, 16:24], inc=sV)
                    rv.wait(sV)
                    for sg in range(8):
                        rv.op("tensor_scalar",
                              scrA[:, sg * 128:(sg + 1) * 128],
                              scrB[:, sg * 128:(sg + 1) * 128],
                              stats[:, 24 + sg:25 + sg], None, ALU.mult)
                    rv.op("tensor_scalar", scrD[:], scrA[:], VELT, None,
                          ALU.mult)
                    rv.op("tensor_tensor", scrC[:], scrD[:], scrA[:],
                          ALU.subtract)
                    rv.op("tensor_tensor", scrC[:], scrD[:], scrC[:],
                          ALU.subtract)
                    for sg in range(8):
                        rv.op("tensor_scalar",
                              scrB[:, sg * 128:(sg + 1) * 128],
                              scrC[:, sg * 128:(sg + 1) * 128],
                              stats[:, 16 + sg:17 + sg], None, ALU.mult)
                writer(i, scrB)

        def make_at_writer():
            marks = []

            def writer(i, src):
                slot = ATOK_O + (i % 2) * 1024
                if len(marks) >= 2:
                    rv.wait(sTR, marks[-2])
                rv.op("tensor_copy", sbf[:, slot:slot + D], src[:], inc=sV)
                rc.wait(sV)
                for ch in range(8):
                    rc.dma(at_sb[:, ch * T + i * 128: ch * T + (i + 1) * 128],
                           sbf[:, slot + ch * 128: slot + (ch + 1) * 128],
                           sTR, transpose=True)
                marks.append(sTR.n)
            return writer

        layernorm_tiles(make_at_writer(), do_quant=True)
        dbg_dump("at", at_sb[:, 0:dbg_cols])
        if upto == "ln1":
            finish()
            return nc

        wstate = {"wf32a": 0, "wf32b": 0, "wst0": 0, "wst1": 0, "wst2": 0}

        def wbuf(buf, dc0, dc1):
            if buf == 2:
                return sbf[:, 29696 + dc0 * 128: 29696 + dc1 * 128]
            return wst[:, buf * 1024 + dc0 * 128: buf * 1024 + dc1 * 128]

        def prep_qkv_style(dram_ap, buf, quant):
            rg.wait(sV, wstate["wf32a"])
            rg.dma(wf32[:, 0:1024].rearrange("p (dc f) -> p dc f", dc=8),
                   dram_ap.rearrange("dc p f -> p dc f"), sLW)
            rv.wait(sLW)
            rv.wait(sT_, wstate["wst%d" % buf])
            if not quant:
                rv.op("tensor_copy", wbuf(buf, 0, 8), wf32[:, 0:1024], inc=sV)
                wstate["wf32a"] = sV.n
                return
            rv.op("tensor_reduce", stats[:, 32:40],
                  wf32[:, 0:1024].rearrange("p (s n) -> p s n", s=8),
                  AX.X, ALU.max, apply_absolute_value=True, inc=sV)
            rv.wait(sV)
            rv.op("tensor_scalar", stats[:, 56:64], stats[:, 32:40],
                  INV448, EPS_S, ALU.mult, ALU.max, inc=sV)
            rv.wait(sV)
            rv.op("reciprocal", stats[:, 32:40], stats[:, 56:64], inc=sV)
            rv.wait(sV)
            for dc in range(8):
                rv.op("tensor_scalar", scrC[:, dc * 128:(dc + 1) * 128],
                      wf32[:, dc * 128:(dc + 1) * 128],
                      stats[:, 32 + dc:33 + dc], None, ALU.mult)
            rv.op("tensor_scalar", scrD[:], scrC[:], VELT, None, ALU.mult)
            rv.op("tensor_tensor", scrB[:], scrD[:], scrC[:], ALU.subtract)
            rv.op("tensor_tensor", scrB[:], scrD[:], scrB[:], ALU.subtract)
            for dc in range(8):
                rv.op("tensor_scalar", wbuf(buf, dc, dc + 1),
                      scrB[:, dc * 128:(dc + 1) * 128],
                      stats[:, 56 + dc:57 + dc], None, ALU.mult,
                      inc=(sV if dc == 7 else None))
            wstate["wf32a"] = sV.n

        def quant_mv_chunk(w, dst):
            rv.op("tensor_reduce", stats[:, 36:40],
                  w.rearrange("p (s n) -> p s n", s=4), AX.X, ALU.max,
                  apply_absolute_value=True, inc=sV)
            rv.wait(sV)
            rv.op("tensor_scalar", stats[:, 40:44], stats[:, 36:40],
                  INV448, EPS_S, ALU.mult, ALU.max, inc=sV)
            rv.wait(sV)
            rv.op("reciprocal", stats[:, 44:48], stats[:, 40:44], inc=sV)
            rv.wait(sV)
            for sg in range(4):
                rv.op("tensor_scalar", scrD[:, sg * 128:(sg + 1) * 128],
                      w[:, sg * 128:(sg + 1) * 128],
                      stats[:, 44 + sg:45 + sg], None, ALU.mult)
            rv.op("tensor_scalar", scrD[:, 512:1024], scrD[:, 0:512],
                  VELT, None, ALU.mult)
            rv.op("tensor_tensor", scrC[:, 0:512], scrD[:, 512:1024],
                  scrD[:, 0:512], ALU.subtract)
            rv.op("tensor_tensor", scrC[:, 0:512], scrD[:, 512:1024],
                  scrC[:, 0:512], ALU.subtract)
            for sg in range(4):
                rv.op("tensor_scalar",
                      dst[:, sg * 128:(sg + 1) * 128],
                      scrC[:, sg * 128:(sg + 1) * 128],
                      stats[:, 40 + sg:41 + sg], None, ALU.mult,
                      inc=(sV if sg == 3 else None))

        at_ready = sTR.n
        first_qkv = [True]
        evacm = {0: 0, 1: 0}
        otall_mark = [0]
        for kt in range(NT):
            rv.op("memset",
                  sbf[:, VTA_O + kt * 96 + 64: VTA_O + kt * 96 + 65], 1.0)
            rv.op("memset",
                  sbf[:, VTB_O + kt * 96 + 64: VTB_O + kt * 96 + 65], 1.0,
                  inc=(sV if kt == NT - 1 else None))

        evacq = {}

        def qkv_mms(oc, buf, stat_mark, dst_off, vtr_marks):
            for q4 in range(QT):
                bank = q4 % 2
                pb = psum[bank]
                for dc in range(8):
                    if dc == 0:
                        rt.wait(sV, stat_mark)
                        rt.wait(sV, evacm[bank])
                        if first_qkv[0]:
                            rt.wait(sTR, at_ready)
                            first_qkv[0] = False
                    rt.op("matmul", pb[:], wbuf(buf, dc, dc + 1),
                          at_sb[:, dc * T + q4 * 512: dc * T + (q4 + 1) * 512],
                          start=(dc == 0), stop=(dc == 7),
                          inc=(sT_ if dc == 7 else None))
                if q4 == QT - 1:
                    wstate["wst%d" % buf] = sT_.n
                rv.wait(sT_)
                if dst_off is None:
                    stg = AT_A + (q4 % 2) * 512
                    if len(vtr_marks) >= 2:
                        rv.wait(sTR, vtr_marks[-2])
                    rv.op("tensor_scalar", sbf[:, stg:stg + 512], pb[:],
                          bq_sb[:, oc:oc + 1], None, ALU.add, inc=sV)
                    evacm[bank] = sV.n
                    rs.wait(sV)
                    for j in range(4):
                        kt = q4 * 4 + j
                        vsrc = sbf[:, stg + j * 128: stg + (j + 1) * 128]
                        rs.dma(sbf[:, VTA_O + kt * 96: VTA_O + kt * 96 + 64],
                               vsrc[0:64, :], sTR, transpose=True)
                        rs.dma(sbf[:, VTB_O + kt * 96: VTB_O + kt * 96 + 64],
                               vsrc[64:128, :], sTR, transpose=True)
                    vtr_marks.append(sTR.n)
                else:
                    rv.op("tensor_scalar",
                          sbf[:, dst_off + q4 * 512: dst_off + (q4 + 1) * 512],
                          pb[:], bq_sb[:, oc:oc + 1], None, ALU.add,
                          inc=sV)
                    evacm[bank] = sV.n
                    evacq[(dst_off, q4)] = sV.n

        def tail_step(q4t, st):
            hh, j = st // 4, st % 4
            kt = q4t * 4 + j
            bank = st % 2
            if st == 0:
                rt.wait(sV, ots_marks[q4t])
            rt.wait(sV, evacm[bank])
            rt.op("transpose", psum[bank][:, 0:65],
                  sf[0:65, OTS_O + hh * 512 + j * 128:
                     OTS_O + hh * 512 + (j + 1) * 128],
                  id_sb[0:65, 0:65], inc=sT_)
            rv.wait(sT_, sT_.n)
            rv.op("tensor_copy",
                  sf[:, OTOK_O + kt * 66: OTOK_O + kt * 66 + 65],
                  psum[bank][:, 0:65], inc=sV)
            evacm[bank] = sV.n
            rv.wait(sV)
            rv.op("reciprocal", stats[:, 52:53],
                  sf[:, OTOK_O + kt * 66 + 64: OTOK_O + kt * 66 + 65],
                  inc=sV)
            rv.wait(sV)
            rv.op("tensor_scalar",
                  sf[:, OPK_O + kt * 128 + hh * 64:
                     OPK_O + kt * 128 + hh * 64 + 64],
                  sf[:, OTOK_O + kt * 66: OTOK_O + kt * 66 + 64],
                  stats[:, 52:53], None, ALU.mult, inc=sV)

        prep_qkv_style(wq_d[0], 0, quant=True)
        nstat = {"q": sV.n}
        prep_qkv_style(wq_d[8], 1, quant=True)
        nstat["k"] = sV.n

        for p in range(NPAIR):
            vtr_marks = []
            ots_marks = []
            stat_q, stat_k = nstat["q"], nstat["k"]
            qkv_mms(p, 0, stat_q, QP_O, vtr_marks)
            prep_qkv_style(wq_d[16 + p], 2, quant=True)
            stat_v = sV.n
            qkv_mms(8 + p, 1, stat_k, KP_O, vtr_marks)

            for idx, src_off in ((0, QP_O), (1, KP_O)):
                for q4 in range(QT):
                    rc.wait(sV, evacq[(src_off, q4)])
                    rc.wait(sT_)
                    rc.op("activation", sf[:, QSQ_O:QSQ_O + 512],
                          sbf[:, src_off + q4 * 512: src_off + (q4 + 1) * 512],
                          AF.Square, scale=1.0, inc=sS)
                    rt.wait(sS)
                    rt.op("matmul", psum[6 + idx][0:2, :], ones2[:, 0:2],
                          sf[:, QSQ_O:QSQ_O + 512],
                          start=(q4 == 0), stop=(q4 == QT - 1), inc=sT_)
            bound_acc = sT_.n
            rv.wait(sT_, bound_acc)
            for bi, bank in enumerate((6, 7)):
                rv.op("tensor_reduce", stats[0:2, 48 + bi:49 + bi],
                      psum[bank][0:2, :], AX.X, ALU.max, inc=sV)
                rv.wait(sV)
            rt.wait(sV)
            rt.op("transpose", psum[2][0:1, 0:128], stats[:, 48:49], id_sb[:],
                  inc=sT_)
            rt.op("transpose", psum[3][0:1, 0:128], stats[:, 49:50], id_sb[:],
                  inc=sT_)
            rv.wait(sT_)
            rv.op("tensor_copy", sf[0:1, 0:128], psum[2][0:1, 0:128])
            rv.op("tensor_copy", sf[0:1, 128:256], psum[3][0:1, 0:128], inc=sV)
            rv.wait(sV)
            rv.op("tensor_tensor", sf[0:1, 240:241], sf[0:1, 0:1],
                  sf[0:1, 1:2], ALU.max)
            rv.op("tensor_tensor", sf[0:1, 242:243], sf[0:1, 128:129],
                  sf[0:1, 129:130], ALU.max, inc=sV)
            rv.wait(sV)
            rv.op("tensor_tensor", sf[0:1, 244:245], sf[0:1, 240:241],
                  sf[0:1, 242:243], ALU.mult, inc=sV)
            rc.wait(sV)
            rc.op("activation", sf[0:1, 245:246], sf[0:1, 244:245],
                  AF.Sqrt, scale=1.0, inc=sS)
            rv.wait(sS)
            rv.op("tensor_scalar", sf[0:1, 245:246], sf[0:1, 245:246],
                  0.125, 80.0, ALU.mult, ALU.min, inc=sV)
            rv.wait(sV)
            rv.op("tensor_scalar", half1[:], sf[0:1, 245:246], -1.0, None,
                  ALU.mult, inc=sV)
            rt.wait(sV)
            rt.op("matmul", psum[6][:, 8:9], on1_sb[:], half1[:],
                  start=True, stop=True, inc=sT_)
            rv.wait(sT_)
            rv.op("tensor_copy", ngc[:], psum[6][:, 8:9], inc=sV)
            ngc_mark = sV.n

            qkv_mms(16 + p, 2, stat_v, None, vtr_marks)
            vtr_mark = sTR.n
            dbg_dump(f"qkvq{p}", sbf[:, QP_O:QP_O + min(dbg_cols, 2048)])
            dbg_dump(f"qkvv{p}", sbf[:, VTA_O:VTA_O + min(dbg_cols, 1536)])
            if upto == "qkv0" and p == 0:
                finish()
                return nc
            if upto == "bound0" and p == 0:
                dbg_dump("c1row", sf[:, 0:256])
                dbg_dump("ngcd", ngc[:])
                dbg_dump("nrm", stats[:, 48:50])
                finish()
                return nc

            for q4 in range(QT):
                av_free_mark = sV.n
                exp_marks = []

                def emit_scores(kt):
                    sbA = psum[2 + (kt % 2)]
                    sbB = psum[6 + (kt % 2)]
                    if kt >= 2:
                        rt.wait(sS, exp_marks[kt - 2])
                    else:
                        rt.wait(sV, av_free_mark)
                    rt.op("matmul", sbA[:],
                          sbf[0:64, KP_O + kt * 128: KP_O + (kt + 1) * 128],
                          sbf[0:64, QP_O + q4 * 512: QP_O + (q4 + 1) * 512],
                          start=True, stop=True, tile_position=(0, 0))
                    rt.op("matmul", sbB[:],
                          sbf[64:128, KP_O + kt * 128: KP_O + (kt + 1) * 128],
                          sbf[64:128, QP_O + q4 * 512: QP_O + (q4 + 1) * 512],
                          start=True, stop=True, tile_position=(64, 0),
                          inc=sT_)

                def emit_exp(kt):
                    sbA = psum[2 + (kt % 2)]
                    sbB = psum[6 + (kt % 2)]
                    rc.wait(sT_)
                    if kt == 0:
                        rc.wait(sV, ngc_mark)
                        rc.wait(sTR, vtr_mark)
                    rc.op("activation",
                          sbf[:, AT_A + (kt % 2) * 512: AT_A + (kt % 2) * 512 + 512],
                          sbA[:], AF.Exp, scale=0.125, bias=ngc[:])
                    rc.op("activation",
                          sbf[:, AT_B + (kt % 2) * 512: AT_B + (kt % 2) * 512 + 512],
                          sbB[:], AF.Exp, scale=0.125, bias=ngc[:], inc=sS)
                    exp_marks.append(sS.n)

                def emit_av(kt):
                    rt.wait(sS, exp_marks[kt])
                    rt.op("matmul", psum[4][0:65, :],
                          sbf[:, VTA_O + kt * 96: VTA_O + kt * 96 + 65],
                          sbf[:, AT_A + (kt % 2) * 512: AT_A + (kt % 2) * 512 + 512],
                          start=(kt == 0), stop=(kt == NT - 1),
                          skip_group_check=True)
                    rt.op("matmul", psum[5][0:65, :],
                          sbf[:, VTB_O + kt * 96: VTB_O + kt * 96 + 65],
                          sbf[:, AT_B + (kt % 2) * 512: AT_B + (kt % 2) * 512 + 512],
                          start=(kt == 0), stop=(kt == NT - 1),
                          skip_group_check=True,
                          inc=(sT_ if kt == NT - 1 else None))

                emit_scores(0)
                emit_exp(0)
                for kt in range(1, NT):
                    emit_scores(kt)
                    emit_av(kt - 1)
                    emit_exp(kt)
                    if q4 >= 1 and 1 <= kt <= 8:
                        tail_step(q4 - 1, kt - 1)
                emit_av(NT - 1)
                rv.wait(sT_)
                rv.op("tensor_copy", sf[0:65, OTS_O:OTS_O + 512],
                      psum[4][0:65, :])
                rv.op("tensor_copy", sf[0:65, OTS_O + 512:OTS_O + 1024],
                      psum[5][0:65, :], inc=sV)
                ots_marks.append(sV.n)
                if upto == "av0" and p == 0 and q4 == 0:
                    dbg_dump("oraw", sf[:, OTS_O:OTS_O + 1024])
                    finish()
                    return nc
                if upto == "otr0" and p == 0 and q4 == 0:
                    for st in range(8):
                        tail_step(0, st)
                    dbg_dump("opkq", sf[:, OPK_O:OPK_O + 512])
                    dbg_dump("otokq", sf[:, OTOK_O:OTOK_O + 1056])
                    finish()
                    return nc
            for st in range(8):
                tail_step(QT - 1, st)
            dbg_dump(f"opk{p}", sf[:, OPK_O:OPK_O + min(dbg_cols, 2048)])

            if p + 1 < NPAIR:
                prep_qkv_style(wq_d[p + 1], 0, quant=True)
                nstat["q"] = sV.n
                prep_qkv_style(wq_d[8 + p + 1], 1, quant=True)
                nstat["k"] = sV.n

            opk = sf[:, OPK_O:OPK_O + 2048]
            rv.op("tensor_reduce", stats[:, 8:24],
                  opk.rearrange("p (s n) -> p s n", s=16), AX.X, ALU.max,
                  apply_absolute_value=True, inc=sV)
            rv.wait(sV)
            rv.op("tensor_scalar", stats[:, 24:40], stats[:, 8:24],
                  INV448, EPS_S, ALU.mult, ALU.max, inc=sV)
            rv.wait(sV)
            rv.op("reciprocal", stats[:, 40:56], stats[:, 24:40], inc=sV)
            rv.wait(sV)
            for sg in range(16):
                dsty = (sf[:, OTOK_O + sg * 128: OTOK_O + (sg + 1) * 128]
                        if sg < 8 else
                        sf[:, OTS_O + (sg - 8) * 128: OTS_O + (sg - 7) * 128])
                rv.op("tensor_scalar", dsty, opk[:, sg * 128:(sg + 1) * 128],
                      stats[:, 40 + sg:41 + sg], None, ALU.mult)
            rv.wait(sTR, otall_mark[0])
            for hb, yo in ((0, OTOK_O), (1, OTS_O)):
                ysl = sf[:, yo:yo + 1024]
                rv.op("tensor_scalar", scrD[:], ysl, VELT, None, ALU.mult)
                rv.op("tensor_tensor", scrC[:], scrD[:], ysl, ALU.subtract)
                rv.op("tensor_tensor", scrC[:], scrD[:], scrC[:], ALU.subtract)
                for sg in range(8):
                    g = hb * 8 + sg
                    rv.op("tensor_scalar",
                          sbf[:, OTQ_O + g * 128: OTQ_O + (g + 1) * 128],
                          scrC[:, sg * 128:(sg + 1) * 128],
                          stats[:, 24 + g:25 + g], None, ALU.mult,
                          inc=(sV if g == 15 else None))
            rs.wait(sV)
            for tt in range(NT):
                rs.dma(sbf[:, OTALL_O + p * T + tt * 128:
                           OTALL_O + p * T + (tt + 1) * 128],
                       sbf[:, OTQ_O + tt * 128: OTQ_O + (tt + 1) * 128],
                       sTR, transpose=True)
            otall_mark[0] = sTR.n
            dbg_dump(f"otq{p}", sbf[:, OTQ_O:OTQ_O + min(dbg_cols, 2048)])
            if upto == "pair0" and p == 0:
                finish()
                return nc

        rv.wait(sT_)
        wall = sbf[:, WOUT_O:WOUT_O + 16 * 512]
        for pc in range(8):
            for dh in range(2):
                k = pc * 2 + dh
                wslot = wf32[:, 1024:1536]
                rs.wait(sV, wstate["wf32b"])
                rs.dma(wslot, wo_d[pc, dh], sLD)
                rv.wait(sLD)
                quant_mv_chunk(wslot, wall[:, k * 512:(k + 1) * 512])
                wstate["wf32b"] = sV.n
        rs.wait(sV)
        rs.dma(scrA[:], bo_d.broadcast_to([128, D]), sLD)
        rv.wait(sLD)
        rv.op("engine_nop", inc=sV)
        wout_mark = sV.n
        rt.wait(sTR)
        for tt in range(NT):
            for dh in range(2):
                pb = psum[(tt * 2 + dh) % 2]
                for pc in range(8):
                    if pc == 0:
                        rt.wait(sV, wout_mark)
                        rt.wait(sV)
                    rt.op("matmul", pb[:],
                          sbf[:, OTALL_O + pc * T + tt * 128:
                              OTALL_O + pc * T + (tt + 1) * 128],
                          wall[:, (pc * 2 + dh) * 512:(pc * 2 + dh + 1) * 512],
                          start=(pc == 0), stop=(pc == 7),
                          inc=(sT_ if pc == 7 else None))
                rv.wait(sT_)
                rv.op("tensor_tensor", sf[:, FFO_O:FFO_O + 512], pb[:],
                      scrA[:, dh * 512:(dh + 1) * 512], ALU.add)
                rv.op("tensor_tensor",
                      x_sb[:, tt * D + dh * 512: tt * D + dh * 512 + 512],
                      x_sb[:, tt * D + dh * 512: tt * D + dh * 512 + 512],
                      sf[:, FFO_O:FFO_O + 512], ALU.add, inc=sV)
        dbg_dump("x1", x_sb[:, 0:dbg_cols])
        if upto == "oproj":
            finish()
            return nc

        rc.wait(sT_)
        layernorm_tiles(make_at_writer(), do_quant=False)
        dbg_dump("bt", at_sb[:, 0:dbg_cols])
        if upto == "ln2":
            finish()
            return nc

        def PW(s):
            return sf[:, PW_O + s * 512: PW_O + (s + 1) * 512]
        H256, T3, FLM, FF, CH, FCH, F8, E1, E2, ACC = range(10)

        rv.wait(sTR)
        rs.wait(sV)
        rs.dma(sf[:, 4096:5120], b2_d.broadcast_to([128, D]), sLD)
        rv.wait(sLD)
        rv.op("tensor_scalar", sf[:, 4096:5120], sf[:, 4096:5120], 256.0,
              None, ALU.mult, inc=sV)
        first_ffn = True
        st_marks = []
        for half in range(2):
            tbase = half * 1024
            def slot(b, s):
                if b == 0:
                    return sf[:, s * 512:(s + 1) * 512]
                t = (scrA, scrB, scrC, scrD)[s // 2]
                return t[:, (s % 2) * 512:(s % 2) * 512 + 512]

            endm = {}
            u1m = {}
            castm = {}
            chunks = [(fc, tq2) for fc in range(32) for tq2 in range(2)]

            def emit_mm(n):
                fc, tq2 = chunks[n]
                if tq2 == 0:
                    prep_qkv_style(w1_d[fc], fc % 2, quant=False)
                    castm[fc] = sV.n
                pb = psum[n % 2]
                for dc in range(8):
                    if dc == 0:
                        rt.wait(sV, castm[fc])
                        rt.wait(sTR)
                        if n >= 2:
                            rt.wait(sV, u1m[n - 2])
                        if n < 2:
                            rt.wait(sV)
                    rt.op("matmul", pb[:],
                          wst[:, (fc % 2) * 1024 + dc * 128:
                              (fc % 2) * 1024 + (dc + 1) * 128],
                          at_sb[:, dc * T + tbase + tq2 * 512:
                                dc * T + tbase + (tq2 + 1) * 512],
                          start=(dc == 0), stop=(dc == 7),
                          inc=(sT_ if dc == 7 else None))
                if tq2 == 1:
                    wstate["wst%d" % (fc % 2)] = sT_.n
                return sT_.n

            def emit_chunk(n, mm_mark):
                fc, tq2 = chunks[n]
                b = n % 2
                S_ = lambda s: slot(b, s)
                pb = psum[n % 2]
                rv.wait(sT_, mm_mark)
                if n == 0:
                    rv.wait(sST)
                if n >= 2:
                    ps_, pg = endm[n - 2]
                    rv.wait(sS, ps_)
                    rv.wait(sG, pg)
                rv.op("tensor_scalar", S_(0), pb[:], b1c_sb[:, fc:fc + 1],
                      None, ALU.add, inc=sV)
                u1m[n] = sV.n
                rc.wait(sV, u1m[n])
                rc.op("activation", S_(1), S_(0), AF.Copy, scale=256.0,
                      bias=MAGIC, inc=sS)
                rc.op("activation", S_(2), S_(1), AF.Copy, scale=1.0 / 64.0,
                      bias=-196592.484375, inc=sS)
                m_t3 = sS.n
                rv.wait(sS, m_t3)
                rv.op("tensor_scalar", S_(1), S_(1), MAGIC, None, ALU.subtract)
                rv.op("tensor_scalar", S_(3), S_(2), MAGIC, None, ALU.add)
                rv.op("tensor_scalar", S_(3), S_(3), MAGIC + 16.0, None,
                      ALU.subtract)
                rv.op("scalar_tensor_tensor", S_(4), S_(2), 15.515625, S_(3),
                      ALU.subtract, ALU.subtract)
                rv.op("tensor_scalar", S_(5), S_(3), 0.125, None, ALU.mult,
                      inc=sV)
                m_fe = sV.n
                rc.wait(sV, m_fe)
                rc.op("activation", S_(0), S_(4), AF.Copy, scale=0.125,
                      bias=0.0, inc=sS)
                m_f8 = sS.n
                rc.op("activation", S_(6), S_(3), AF.Copy, scale=0.125,
                      bias=0.125)
                rc.op("activation", S_(2), S_(5), AF.Erf, scale=SQRT2)
                rc.op("activation", S_(3), S_(6), AF.Erf, scale=SQRT2, inc=sS)
                m_e2 = sS.n
                rg.wait(sV, m_fe)
                rg.op("tensor_tensor", S_(7), S_(4), S_(5), ALU.mult, inc=sG)
                m_fch = sG.n
                rg.wait(sS, m_f8)
                rg.op("tensor_tensor", S_(4), S_(7), S_(0), ALU.add, inc=sG)
                m_bh = sG.n
                rv.wait(sG, m_fch)
                rv.wait(sS, m_e2)
                rv.op("tensor_tensor", S_(6), S_(5), S_(7), ALU.subtract)
                rv.op("tensor_tensor", S_(5), S_(6), S_(2), ALU.mult)
                rv.wait(sG, m_bh)
                rv.op("tensor_tensor", S_(7), S_(4), S_(3), ALU.mult)
                rv.op("tensor_tensor", S_(2), S_(5), S_(7), ALU.add)
                rv.op("scalar_tensor_tensor", S_(4), S_(1), 2.0 ** -9, S_(2),
                      ALU.mult, ALU.add, inc=sV)
                m_y = sV.n
                rc.wait(sV, m_y)
                rc.op("activation", S_(1), S_(4), AF.Copy, scale=256.0,
                      bias=MAGIC, inc=sS)
                rc.op("activation",
                      sbf[:, YQ_O + fc * 1024 + tq2 * 512:
                          YQ_O + fc * 1024 + (tq2 + 1) * 512],
                      S_(1), AF.Copy, scale=2.0 ** -8, bias=-49152.0, inc=sS)
                endm[n] = (sS.n, sG.n)

            mm_marks = {0: emit_mm(0)}
            for n in range(len(chunks)):
                if n + 1 < len(chunks):
                    mm_marks[n + 1] = emit_mm(n + 1)
                emit_chunk(n, mm_marks[n])
            yq_mark = sS.n
            if half == 0:
                dbg_dump("yq", sbf[:, YQ_O:YQ_O + dbg_cols])
                if upto == "ffn1":
                    finish()
                    return nc
            for dh in range(2):
                ev_mark = sV.n
                w2marks = []
                for fcw in range(32):
                    wslot = wf32[:, 1024:1536]
                    rs.wait(sV, wstate["wf32b"])
                    rs.dma(wslot, w2_d[fcw, dh], sLD)
                    rv.wait(sLD)
                    if fcw >= 2:
                        rv.wait(sT_, w2marks[fcw - 2])
                    rv.op("tensor_copy",
                          wst[:, (fcw % 2) * 512:(fcw % 2) * 512 + 512],
                          wslot, inc=sV)
                    wstate["wf32b"] = sV.n
                    wmark = sV.n
                    for tt8 in range(8):
                        if tt8 == 0:
                            rt.wait(sV, wmark)
                            rt.wait(sS, yq_mark)
                            if fcw == 0:
                                rt.wait(sV, ev_mark)
                        rt.op("matmul", psum[tt8][:],
                              sbf[:, YQ_O + fcw * 1024 + tt8 * 128:
                                  YQ_O + fcw * 1024 + (tt8 + 1) * 128],
                              wst[:, (fcw % 2) * 512:(fcw % 2) * 512 + 512],
                              start=(fcw == 0), stop=(fcw == 31),
                              skip_group_check=True,
                              inc=(sT_ if tt8 == 7 else None))
                    w2marks.append(sT_.n)
                rv.wait(sT_)
                for tt8 in range(8):
                    gt = half * 8 + tt8
                    s_u = PW(tt8 % 2)
                    s_o = PW(2 + tt8 % 2)
                    if len(st_marks) >= 2:
                        rv.wait(sST, st_marks[-2])
                    rv.op("scalar_tensor_tensor", s_u, psum[tt8][:], 256.0,
                          sf[:, 4096 + dh * 512:4096 + (dh + 1) * 512],
                          ALU.mult, ALU.add)
                    rv.op("tensor_scalar", s_u, s_u, MAGIC, None, ALU.add)
                    rv.op("tensor_scalar", s_u, s_u, MAGIC, 2.0 ** -8,
                          ALU.subtract, ALU.mult)
                    rv.op("tensor_tensor", s_o, s_u,
                          x_sb[:, gt * D + dh * 512: gt * D + dh * 512 + 512],
                          ALU.add, inc=sV)
                    rs.wait(sV)
                    rs.dma(out_d[gt][:, dh * 512:(dh + 1) * 512], s_o, sST)
                    st_marks.append(sST.n)

        finish()

    return nc


_NC_CACHE = {}


def _get_nc():
    if "nc" not in _NC_CACHE:
        _NC_CACHE["nc"] = build()
    return _NC_CACHE["nc"]


def _layout(L0_attn_x, L0_attn_qkv_kernel, L0_attn_qkv_bias, L0_attn_out_kernel,
            L0_attn_out_bias, L0_ffn_w1, L0_ffn_b1, L0_ffn_w2, L0_ffn_b2):
    x = np.ascontiguousarray(np.asarray(L0_attn_x, dtype=np.float32))
    wq = np.ascontiguousarray(
        np.asarray(L0_attn_qkv_kernel, np.float32)
        .reshape(8, 128, 24, 128).transpose(2, 0, 1, 3))
    bq = np.ascontiguousarray(np.asarray(L0_attn_qkv_bias, np.float32).reshape(24, 128).T)
    wo = np.ascontiguousarray(
        np.asarray(L0_attn_out_kernel, np.float32)
        .reshape(8, 128, 2, 512).transpose(0, 2, 1, 3))
    bo = np.ascontiguousarray(np.asarray(L0_attn_out_bias, np.float32).reshape(1, 1024))
    w1 = np.ascontiguousarray(
        np.asarray(L0_ffn_w1, np.float32)
        .reshape(8, 128, 32, 128).transpose(2, 0, 1, 3))
    b1 = np.ascontiguousarray(np.asarray(L0_ffn_b1, np.float32).reshape(32, 128).T)
    w2 = np.ascontiguousarray(
        np.asarray(L0_ffn_w2, np.float32)
        .reshape(32, 128, 2, 512).transpose(0, 2, 1, 3))
    b2 = np.ascontiguousarray(np.asarray(L0_ffn_b2, np.float32).reshape(1, 1024))
    in_maps = []
    for c in range(8):
        in_maps.append(dict(
            x=np.ascontiguousarray(x[c].reshape(NT, 128, D)),
            wqkv=wq, bqkv=bq, wout=wo, bout=bo,
            w1=w1, b1=b1, w2=w2, b2=b2))
    return in_maps


def kernel(**inputs):
    in_maps = _layout(**inputs)
    nc = _get_nc()
    res = run_bass_kernel_spmd(nc, in_maps, core_ids=list(range(8)))
    out = np.stack([np.asarray(r["out"], np.float32).reshape(T, D)
                    for r in res.results])
    return out
